# version 11
# baseline (speedup 1.0000x reference)
import numpy as np
import ml_dtypes
import concourse.bass as bass
import concourse.mybir as mybir
from concourse.bass_utils import run_bass_kernel_spmd

F32 = mybir.dt.float32
I32 = mybir.dt.int32
BF16 = mybir.dt.bfloat16
U8 = mybir.dt.uint8
FP8 = mybir.dt.float8e4
ALU = mybir.AluOpType
ACTF = mybir.ActivationFunctionType
AXX = mybir.AxisListType.X

T = 2048
E = 1024
H = 16
D = 64
NT = T // 128
GT = 2
NG = NT // GT
NK = E // 128
NO = 2
RT = 10

ENGS = ("sync", "tensor", "vector", "scalar", "gpsimd")
PHASE = 4


class Sched:
    def __init__(self):
        self.items = []
        self.M = {}

    def add(self, eng, fn, waits=(), signal=None, inc=1, sem=None):
        self.items.append((eng, tuple(waits), signal, inc, fn, sem or eng))

    def resolve(self):
        val = {}
        for eng, waits, signal, inc, fn, sk in self.items:
            if signal is not None:
                val[sk] = val.get(sk, 0) + inc
                names = signal if isinstance(signal, tuple) else (signal,)
                for nm in names:
                    assert nm not in self.M, f"dup signal {nm}"
                    self.M[nm] = (sk, val[sk])
        for eng, waits, signal, inc, fn, sk in self.items:
            for w in waits:
                assert w in self.M, f"unresolved wait {w} (engine {eng})"

    def emit(self, block, sems):
        per_eng = {e: [] for e in ENGS}
        for it in self.items:
            per_eng[it[0]].append(it)

        def make_runner(eng_name):
            def run(e):
                for _, waits, signal, inc, fn, sk in per_eng[eng_name]:
                    for w in waits:
                        psk, v = self.M[w]
                        e.wait_ge(sems[psk], v)
                    insts = fn(e)
                    if signal is not None:
                        if isinstance(insts, list):
                            for inst in insts:
                                inst.then_inc(sems[sk], 16)
                        else:
                            insts.then_inc(sems[sk], 1)
            return run

        for name in ENGS:
            if per_eng[name]:
                getattr(block, name)(make_runner(name))


def build_program():
    nc = bass.Bass("TRN2", target_bir_lowering=False, debug=False,
                   detect_race_conditions=False)

    x_h = nc.dram_tensor("x", [128, NT * E], F32, kind="ExternalInput").ap()
    wT_h = [nc.dram_tensor(n, [2 * E, E // 2], F32, kind="ExternalInput").ap()
            for n in ("wqT", "wkT", "wvT", "wpT")]
    w8T_h = nc.dram_tensor("w8T", [E, E], U8, kind="ExternalInput").ap()
    bp_h = nc.dram_tensor("bp", [1, E], F32, kind="ExternalInput").ap()
    fb_h = nc.dram_tensor("fb", [1, E], F32, kind="ExternalInput").ap()
    ws_h = nc.dram_tensor("ws", [128, 1], F32, kind="ExternalInput").ap()
    cp_h = nc.dram_tensor("cp", [128, 4], F32, kind="ExternalInput").ap()
    mask_h = nc.dram_tensor("maskT", [128, 128], BF16, kind="ExternalInput").ap()
    idn_h = nc.dram_tensor("idn", [128, 128], F32, kind="ExternalInput").ap()
    sel_h = nc.dram_tensor("sel", [128, 8, 16], BF16, kind="ExternalInput").ap()
    ones_h = nc.dram_tensor("ones", [1, 128], BF16, kind="ExternalInput").ap()
    out_h = nc.dram_tensor("out", [128, NT * E], F32,
                           kind="ExternalOutput").ap()

    A = nc.alloc_sbuf_tensor
    x_sb = A("x_sb", [128, RT, E], F32)
    w_sb = [A(f"w{n}_sb", [128, NK, E], BF16) for n in "qkvp"]
    w8_sb = A("w8_sb", [128, NK, E], FP8)
    w8u = w8_sb.ap().bitcast(U8)
    qf8a = A("qf8a", [128, NK, GT * 128], FP8)
    qf8b = A("qf8b", [128, NK, GT * 128], FP8)
    qf8w = {"f1": qf8a, "f2": qf8b}
    ralpha = A("ralpha", [128, NT], BF16)
    iarb = A("iarb", [1, 2, GT * 128], BF16)
    stgS = A("stgS", [128, 544], F32)
    stg2 = A("stg2", [128, 512], F32)
    stg = [stgS[:, 0:512], stg2[:, :]]

    ws_sb = A("ws_sb", [128, 1], F32)
    cp_sb = A("cp_sb", [128, 4], F32)
    mask_sb = A("mask_sb", [128, 128], BF16)
    idn_sb = A("idn_sb", [128, 128], F32)
    sel_sb = A("sel_sb", [128, 8, 16], BF16)
    ones_sb = A("ones_sb", [1, 128], BF16)
    bp_sb = A("bp_sb", [1, E], BF16)
    fbh_sb = A("fbh_sb", [1, E], BF16)

    sxA = A("sxA", [128, NT], F32)
    sxB = A("sxB", [128, NT], F32)
    sxC = A("sxC", [128, NT], F32)
    mu_c = A("mu_c", [128, NT], F32)
    ssq_c = A("ssq_c", [128, NT], F32)
    mxa = A("mxa", [128, NT], F32)
    mxb = A("mxb", [128, NT], F32)
    rstd_c = A("rstd_c", [128, NT], F32)
    amax_c = A("amax_c", [128, NT], F32)
    rq_c = A("rq_c", [128, NT], F32)
    alpha_c = A("alpha_c", [128, NT], F32)
    tA = A("tA", [128, NT], F32)
    tB = A("tB", [128, NT], F32)

    q_dev = A("q_dev", [128, E], F32)
    q_gh = A("q_gh", [128, E], I32)
    q_t7 = A("q_t7", [128, E], F32)
    q_ub = A("q_ub", [128, E], I32)
    q_a1 = A("q_a1", [128, E], F32)
    q_a2 = A("q_a2", [128, E], BF16)
    sq_scr = q_a2

    qtm = A("qtm", [128, GT, E], BF16)
    fm_sh = A("fm_sh", [128, NK, GT * 128], BF16)
    qfm = fm_sh
    ln2fm = A("ln2fm_sb", [128, NK, GT * 128], BF16)
    y_fm = A("y_fm_sb", [128, NK, GT * 128], BF16)
    phq_fm = A("phq_fm_sb", [128, NK, GT * 128], BF16)

    ln2tm = A("ln2tm", [128, GT, E], BF16)
    y_tm = A("y_tm_sb", [128, GT, E], BF16)
    e1_sb = ln2tm[:, 0, :]
    r_sb = ln2tm[:, 1, :]
    ph_tm2 = A("ph_tm2", [128, 2, GT, E], BF16)
    phq_tm = ph_tm2[:, 0]
    phk_tm = ph_tm2[:, 1]
    phk_fm = A("phk_fm", [128, NK, GT * 128], BF16)
    vex = A("vex", [128, 2, H, D + 1], BF16)
    atm = A("atm_sb", [128, 16, 128], BF16)
    u_sb = A("u_sb", [128, 8, 128], BF16)
    S_sb = stgS[:, 0:520].rearrange("p (a b) -> p a b", b=D + 1)
    S_bf = A("S_bf", [128, 8, D + 1], BF16)
    den_f = stg2[0:16, 0:128]
    rden = A("rden", [128, GT, 16], F32)

    d_i = q_ub
    d_e = q_gh
    d_f = q_t7[:, :].bitcast(I32)
    d_s = q_dev
    d_t1 = q_a1
    d_t2 = q_a2
    d_p2 = q_ub

    psA = nc.alloc_psum_tensor("psA", [128, 512], F32)
    psB = nc.alloc_psum_tensor("psB", [128, 512], F32)
    psK = nc.alloc_psum_tensor("psK", [128, 512], F32)
    psT1 = nc.alloc_psum_tensor("psT1", [128, 4, 128], F32)
    psT2 = nc.alloc_psum_tensor("psT2", [128, 4, 128], F32)
    psW = nc.alloc_psum_tensor("psW", [128, 4, 128], F32)
    psN = nc.alloc_psum_tensor("psN", [128, 8, D], F32)
    psD = nc.alloc_psum_tensor("psD", [128, 512], F32)
    psAB = [psA, psB]
    psT = [psT1, psT2]

    S = Sched()

    ss_sem = {}
    ss_val = {}

    def ssync(e, inst):
        nm = str(e.engine)
        if nm not in ss_sem:
            ss_sem[nm] = nc.alloc_semaphore(f"ss_{len(ss_sem)}")
            ss_val[nm] = 0
        ss_val[nm] += 1
        inst.then_inc(ss_sem[nm], 1)
        e.wait_ge(ss_sem[nm], ss_val[nm])
        return inst

    S.add("sync", lambda e: [
        e.dma_start(out=ws_sb[:, :], in_=ws_h),
        e.dma_start(out=cp_sb[:, :], in_=cp_h),
        e.dma_start(out=mask_sb[:, :], in_=mask_h),
        e.dma_start(out=idn_sb[:, :], in_=idn_h),
        e.dma_start(out=sel_sb[:, :, :], in_=sel_h),
        e.dma_start(out=ones_sb[:, :], in_=ones_h),
        e.dma_start(out=q_dev[0:1, 0:E], in_=bp_h),
        e.dma_start(out=q_t7[0:1, 0:E], in_=fb_h),
    ], signal="smallc", inc=8 * 16)

    def emit_xload(g, waits=()):
        s0 = (g * GT) % RT
        S.add("sync", (lambda e, g=g, s0=s0: [
            e.dma_start(out=x_sb[:, s0:s0 + GT, :],
                        in_=x_h.rearrange("p (t e) -> p t e", e=E)[
                            :, g * GT:(g + 1) * GT, :])]),
            waits=waits, signal=f"x_in_g{g}", inc=16)

    sums_done = set()

    def emit_sums(g):
        if g in sums_done:
            return
        sums_done.add(g)

        def sums_g(e, g=g):
            for ti in range(GT):
                t = g * GT + ti
                e.reduce_sum(sxA[:, t:t + 1], x_sb[:, t % RT, :], AXX)
            return e.engine_nop()
        S.add("vector", sums_g, waits=(f"x_in_g{g}",), signal=f"sx1_g{g}")

    for g in range(min(NG, 2)):
        emit_xload(g)
    S.add("sync", lambda e: [
        e.dma_start(out=w8u[:, :, :],
                    in_=w8T_h.rearrange("(c p) o -> p c o", p=128)),
    ], waits=("smallc",), signal="consts", inc=16)
    for g in range(2, min(NG, RT // GT)):
        emit_xload(g)

    def init_v(e):
        e.tensor_copy(bp_sb[0:1, :], q_dev[0:1, 0:E])
        e.tensor_scalar(fbh_sb[0:1, :], q_t7[0:1, 0:E], 0.5, None, ALU.mult)
        return e.memset(vex[:, :, :, D:D + 1], 1.0)
    S.add("vector", init_v, waits=("smallc",), signal="init_v")

    def make_dec(c):
        def dec(e):
            e.tensor_copy(d_i[:, :], w8u[:, c, :])
            e.tensor_scalar(d_e[:, :], d_i[:, :], 0x78, 3, ALU.bitwise_and,
                            ALU.logical_shift_right)
            e.tensor_scalar(d_f[:, :], d_i[:, :], 0x7, None, ALU.bitwise_and)
            e.tensor_scalar(d_p2[:, :], d_i[:, :], 7, None,
                            ALU.logical_shift_right)
            e.tensor_scalar(d_s[:, :], d_p2[:, :], -1.0, 0.5, ALU.mult,
                            ALU.add)
            e.tensor_scalar(d_t1[:, :], d_e[:, :], 1.0, 8.0, ALU.min,
                            ALU.mult)
            e.tensor_tensor(d_t1[:, :], d_t1[:, :], d_f[:, :], ALU.add)
            e.tensor_scalar(d_t2[:, :], d_e[:, :], 1.0, 117.0, ALU.max,
                            ALU.add)
            e.tensor_scalar(d_p2[:, :], d_t2[:, :], 8388608.0, None, ALU.mult)
            e.tensor_tensor(d_t1[:, :], d_t1[:, :], d_p2[:, :].bitcast(F32),
                            ALU.mult)
            return e.tensor_tensor(w8_sb[:, c, :], d_t1[:, :], d_s[:, :],
                                   ALU.mult)
        return dec

    wload_items = []
    wcvt_items = []
    for wi in range(4):
        for c in range(NK):
            for hf in range(2):
                li = (wi * NK + c) * 2 + hf
                wts = ["init_v"] if li < 2 else [f"wcvt_{li - 2}"]
                wload_items.append((li, wts, (lambda e, wi=wi, c=c, hf=hf,
                                              li=li: [
                    e.dma_start(out=stg[li % 2][:, :],
                                in_=wT_h[wi][(c * 2 + hf) * 128:
                                             (c * 2 + hf + 1) * 128, :])])))
                wcvt_items.append((li, (lambda e, wi=wi, c=c, hf=hf, li=li:
                                        e.copy(
                    w_sb[wi][:, c, hf * 512:(hf + 1) * 512],
                    stg[li % 2][:, :]))))

    big_i = [0]

    def big_tile(sig, lhs_fn, rhs_fn, waits=(), outer=None):
        i = big_i[0]
        big_i[0] += 1
        ps = psAB[i % 2]
        w = list(waits)
        if i >= 2:
            w.append(f"bigfree_{i - 2}")

        def fn(e, ps=ps):
            mm = None
            for k in range(NK):
                mm = e.matmul(ps[:, :], lhs_fn(k), rhs_fn(k), start=(k == 0),
                              stop=(k == NK - 1 and outer is None))
            if outer is not None:
                lhs, rhs = outer()
                mm = e.matmul(ps[:, :], lhs, rhs, start=False, stop=True)
            return mm
        S.add("tensor", fn, waits=w, signal=sig)
        return ps, i

    prevq = {"sig": None}

    def ffn_quant_tile(t, which, sx_src, extra_wait=()):
        ti = t % GT
        tg = f"{which}_{t}"

        def stats1(e):
            ssync(e, e.tensor_scalar(mu_c[:, t:t + 1], sx_src[:, t:t + 1],
                                     1.0 / E, None, ALU.mult))
            e.tensor_scalar(q_dev[:, :], x_sb[:, t % RT, :],
                            mu_c[:, t:t + 1],
                            None, ALU.subtract, ALU.min,
                            accum_out=mxa[:, t:t + 1])
            return e.tensor_scalar(q_dev[:, :], x_sb[:, t % RT, :],
                                   mu_c[:, t:t + 1], None, ALU.subtract,
                                   ALU.max, accum_out=mxb[:, t:t + 1])
        S.add("vector", stats1, waits=tuple(extra_wait), signal=f"dev_{tg}")

        S.add("scalar", (lambda e: e.activation(
            sq_scr[:, :], q_dev[:, :], ACTF.Square,
            accum_out=ssq_c[:, t:t + 1])), waits=(f"dev_{tg}",),
            signal=f"ssq_{tg}")

        def stats2(e):
            ssync(e, e.tensor_scalar(tA[:, t:t + 1], mxa[:, t:t + 1], -1.0,
                                     None, ALU.mult))
            ssync(e, e.tensor_tensor(amax_c[:, t:t + 1], tA[:, t:t + 1],
                                     mxb[:, t:t + 1], ALU.max))
            ssync(e, e.tensor_scalar(tA[:, t:t + 1], amax_c[:, t:t + 1],
                                     1e-30, None, ALU.max))
            ssync(e, e.reciprocal(rq_c[:, t:t + 1], tA[:, t:t + 1]))
            ssync(e, e.tensor_scalar(rq_c[:, t:t + 1], rq_c[:, t:t + 1],
                                     448.0, None, ALU.mult))
            e.tensor_scalar(tB[:, t:t + 1], ssq_c[:, t:t + 1],
                            1.0 / E, 1e-5, ALU.mult, ALU.add)
            return e.tensor_scalar(q_dev[:, :], q_dev[:, :],
                                   rq_c[:, t:t + 1], None, ALU.mult)
        S.add("vector", stats2, waits=(f"ssq_{tg}",),
              signal=(f"st2_{tg}", f"y_{tg}"))

        def rstd_f(e):
            ssync(e, e.activation(tB[:, t:t + 1], tB[:, t:t + 1], ACTF.Ln))
            return e.activation(rstd_c[:, t:t + 1], tB[:, t:t + 1], ACTF.Exp,
                                bias=0.0, scale=cp_sb[:, 1:2])
        S.add("scalar", rstd_f, waits=(f"st2_{tg}",), signal=f"rstd_{tg}")

        def expo(e):
            zb = q_t7[:, :].bitcast(I32)
            e.tensor_scalar(zb, q_dev[:, :].bitcast(I32), 0x7F800000,
                            None, ALU.bitwise_and)
            e.tensor_scalar(zb, zb, 1015021568.0, None, ALU.max)
            e.tensor_scalar(q_gh[:, :], zb, 33554432.0, None, ALU.subtract)
            e.tensor_scalar(q_a1[:, :].bitcast(I32), q_gh[:, :], -1.0,
                            2122317824.0, ALU.mult, ALU.add)
            e.tensor_tensor(q_t7[:, :], q_dev[:, :], q_a1[:, :], ALU.mult)
            e.tensor_scalar(q_t7[:, :], q_t7[:, :], 80.5, None, ALU.add)
            e.tensor_scalar(q_ub[:, :], q_t7[:, :].bitcast(I32), -131072,
                            None, ALU.bitwise_and)
            return e.tensor_tensor(q_dev[:, :], q_t7[:, :],
                                   q_ub[:, :].bitcast(F32), ALU.subtract)
        S.add("vector", expo, waits=(f"y_{tg}",), signal=f"dd_{tg}")

        def signs(e):
            e.activation(q_a1[:, :], q_dev[:, :], ACTF.Sign)
            return e.activation(q_a2[:, :], q_dev[:, :], ACTF.Sign,
                                bias=cp_sb[:, 0:1], scale=1.0)
        S.add("scalar", signs, waits=(f"dd_{tg}",), signal=f"sgn_{tg}")

        def tail2(e):
            ssync(e, e.tensor_tensor(tA[:, t:t + 1], rstd_c[:, t:t + 1],
                                     amax_c[:, t:t + 1], ALU.mult))
            ssync(e, e.tensor_scalar(tA[:, t:t + 1], tA[:, t:t + 1], 1e-12,
                                     2.0 / 448.0, ALU.max, ALU.mult))
            ssync(e, e.tensor_tensor(alpha_c[:, t:t + 1], tA[:, t:t + 1],
                                     ws_sb[:, 0:1], ALU.mult))
            with nc.allow_low_precision(reason="1/alpha is a bf16 matmul "
                                         "operand (bias outer product)"):
                e.reciprocal(ralpha[:, t:t + 1], alpha_c[:, t:t + 1])
            e.tensor_scalar(q_t7[:, :], q_ub[:, :].bitcast(F32), -80.0,
                            None, ALU.add)
            e.tensor_tensor(q_a1[:, :], q_a1[:, :], q_a2[:, :], ALU.add)
            e.scalar_tensor_tensor(q_t7[:, :], q_a1[:, :], 0.5,
                                   q_t7[:, :], ALU.mult, ALU.add)
            return e.tensor_tensor(qtm[:, ti, :], q_t7[:, :],
                                   q_gh[:, :].bitcast(F32), ALU.mult)
        S.add("vector", tail2, waits=(f"sgn_{tg}", f"rstd_{tg}"),
              signal=(f"tl_{tg}", f"qtm_{tg}"))

    def ffn_mm(t, which, g):
        ti = t % GT
        par = ffn_seq_par[which, g]
        tiles = []
        for o in range(NO):
            sig = f"ffnps_{which}_{t}_{o}"
            ps, i = big_tile(
                sig,
                (lambda k, ti=ti, q8=qf8w[which]:
                 q8[:, k, ti * 128:(ti + 1) * 128]),
                (lambda k, o=o: w8_sb[:, k, o * 512:(o + 1) * 512]),
                waits=(f"qf8c_{which}_g{g}", f"iarow_{which}_g{g}",
                       f"wdec_{NK - 1}", "init_v"),
                outer=(lambda o=o, ti=ti, par=par: (
                    iarb[0:1, par, ti * 128:(ti + 1) * 128],
                    fbh_sb[0:1, o * 512:(o + 1) * 512])))
            tiles.append((sig, ps, i, o))
        return tiles

    ffn_seq = [0]
    ffn_seq_par = {}

    def ffn_qcast(which, g, qf8_free_prev):
        si = ffn_seq[0]
        ffn_seq[0] += 1
        par = si % 2
        ffn_seq_par[which, g] = par
        g0 = g * GT
        S.add("scalar", (lambda e, q8=qf8w[which]: e.copy(
            q8[:, :, :], qfm[:, :, :])),
              waits=(f"qfm_{which}_g{g}",) + qf8_free_prev,
              signal=f"qf8c_{which}_g{g}")

        def iarow(e, g0=g0, par=par):
            return [e.dma_start(out=iarb[0:1, par, ti * 128:(ti + 1) * 128],
                                in_=ralpha[:, g0 + ti:g0 + ti + 1])
                    for ti in range(GT)]
        S.add("gpsimd", iarow,
              waits=tuple(f"qtm_{which}_{g0 + ti}" for ti in range(GT)) +
                    qf8_free_prev,
              signal=f"iarow_{which}_g{g}", inc=GT * 16, sem="gpsimd_dma")

    def ffn_evac(t, which, tiles, sx_next):
        tg = f"{which}_{t}"
        for sig, ps, i, o in tiles:
            def ev(e, ps=ps, o=o, t=t):
                ssync(e, e.scalar_tensor_tensor(
                    x_sb[:, t % RT, o * 512:(o + 1) * 512], ps[:, :],
                    alpha_c[:, t:t + 1],
                    x_sb[:, t % RT, o * 512:(o + 1) * 512], ALU.mult,
                    ALU.add, accum_out=tA[:, t:t + 1]))
                if sx_next is not None:
                    if o == 0:
                        return e.tensor_copy(sx_next[:, t:t + 1],
                                             tA[:, t:t + 1])
                    return e.tensor_tensor(sx_next[:, t:t + 1],
                                           sx_next[:, t:t + 1],
                                           tA[:, t:t + 1], ALU.add)
                return e.tensor_copy(tB[:, t:t + 1], tA[:, t:t + 1])
            S.add("vector", ev, waits=(sig,),
                  signal=(f"bigfree_{i}", f"ffnev_{tg}_{o}"))

    wc_list = list(wcvt_items)
    wl_list = list(wload_items)
    dec_iter = iter(range(NK))
    seqst = {"qtm": None, "qfm": None, "mm": None,
             "mm_f1": None, "mm_f2": None}

    def emit_quant(which, g, sx_src):
        g0 = g * GT
        for ti in range(GT):
            t = g0 + ti
            if which == "f1":
                ew = [f"sx1_g{g}"]
            else:
                ew = [f"pev_{t}_{NO - 1}", f"ssq2_{t}"]
            if seqst["qtm"] is not None:
                ew.append(seqst["qtm"])
            ffn_quant_tile(t, which, sx_src, extra_wait=tuple(ew))
            if which == "f1":
                for _ in range(2):
                    try:
                        c = next(dec_iter)
                    except StopIteration:
                        break
                    S.add("vector", make_dec(c),
                          waits=("consts",) if c == 0 else (),
                          signal=f"wdec_{c}")
        seqst["qtm"] = f"qfm_{which}_g{g}"

    def emit_fT_cast(which, g):
        tw = [f"qtm_{which}_{g * GT}", f"qtm_{which}_{g * GT + 1}"]
        if seqst["qfm"] is not None:
            tw.append(seqst["qfm"])
        S.add("sync", (lambda e, g=g: [
            e.dma_start_transpose(qfm[:, :, tt * 128:(tt + 1) * 128],
                                  qtm[:, tt, :].rearrange(
                                      "p (c m) -> p c m", c=NK))
            for tt in range(GT)]),
            waits=tuple(tw), signal=f"qfm_{which}_g{g}", inc=GT * 16)
        seqst["qfm"] = f"qf8c_{which}_g{g}"
        mmprev = seqst["mm_" + which]
        ffn_qcast(which, g,
                  qf8_free_prev=(mmprev,) if mmprev else ("consts",))

    def emit_fmm(which, g):
        tiles = [(g * GT + ti, ffn_mm(g * GT + ti, which, g))
                 for ti in range(GT)]
        S.add("tensor", lambda e: e.nop(),
              waits=(f"ffnps_{which}_{g * GT + GT - 1}_{NO - 1}",),
              signal=f"ffn_mm_done_{which}_g{g}")
        seqst["mm_" + which] = f"ffn_mm_done_{which}_g{g}"
        return tiles

    def emit_fT_mm(which, g):
        emit_fT_cast(which, g)
        return emit_fmm(which, g)

    last_e1 = {}

    def emit_ln2(g):
        g0 = g * GT
        for ti in range(GT):
            t = g0 + ti

            def dev2(e, t=t):
                ssync(e, e.tensor_scalar(mu_c[:, t:t + 1], sxB[:, t:t + 1],
                                         1.0 / E, None, ALU.mult))
                return e.tensor_scalar(q_dev[:, :], x_sb[:, t % RT, :],
                                       mu_c[:, t:t + 1], None, ALU.subtract)
            S.add("vector", dev2, waits=(f"ffnev_f1_{t}_{NO - 1}",),
                  signal=f"dev2_{t}")
            S.add("scalar", (lambda e, t=t: e.activation(
                sq_scr[:, :], q_dev[:, :], ACTF.Square,
                accum_out=ssq_c[:, t:t + 1])), waits=(f"dev2_{t}",),
                signal=f"ssq2_{t}")

            def var2(e, t=t):
                return e.tensor_scalar(tB[:, t:t + 1], ssq_c[:, t:t + 1],
                                       1.0 / E, 1e-5, ALU.mult, ALU.add)
            S.add("vector", var2, waits=(f"ssq2_{t}",), signal=f"var2_{t}")

            def rstd2(e, t=t):
                ssync(e, e.activation(tB[:, t:t + 1], tB[:, t:t + 1],
                                      ACTF.Ln))
                return e.activation(rstd_c[:, t:t + 1], tB[:, t:t + 1],
                                    ACTF.Exp, bias=0.0, scale=cp_sb[:, 1:2])
            S.add("scalar", rstd2, waits=(f"var2_{t}",), signal=f"rstd2_{t}")

            def mkln2(e, t=t, ti=ti):
                return e.tensor_scalar(ln2tm[:, ti, :], q_dev[:, :],
                                       rstd_c[:, t:t + 1], None, ALU.mult)
            lnw = [f"rstd2_{t}"]
            if g >= 1:
                lnw.append(f"ln2fm_g{g - 1}")
            S.add("vector", mkln2, waits=tuple(lnw), signal=f"ln2_{t}")

    def emit_ln2fm(g):
        g0 = g * GT
        S.add("sync", (lambda e, g=g: [
            e.dma_start_transpose(ln2fm[:, :, tt * 128:(tt + 1) * 128],
                                  ln2tm[:, tt, :].rearrange(
                                      "p (c m) -> p c m", c=NK))
            for tt in range(GT)]),
            waits=(f"ln2_{g0}", f"ln2_{g0 + 1}") +
                  ((f"qkv_done_g{g - 1}",) if g >= 1 else ()),
            signal=f"ln2fm_g{g}", inc=GT * 16)

    def emit_qkv(g):
        g0 = g * GT
        qkv_tiles = {}
        for wi, wn in ((0, "q"), (1, "k"), (2, "v")):
            for ti in range(GT):
                t = g0 + ti
                for o in range(NO):
                    sig = f"{wn}ps_{t}_{o}"
                    ps, i = big_tile(
                        sig,
                        (lambda k, ti=ti: ln2fm[:, k,
                                                ti * 128:(ti + 1) * 128]),
                        (lambda k, wi=wi, o=o: w_sb[wi][
                            :, k, o * 512:(o + 1) * 512]),
                        waits=(f"ln2fm_g{g}", f"wcvt_{(wi * NK + NK) * 2 - 1}"))
                    qkv_tiles[(wn, ti, o)] = (sig, ps, i)
        S.add("tensor", lambda e: e.nop(),
              waits=(f"vps_{g0 + GT - 1}_{NO - 1}",),
              signal=f"qkv_done_g{g}")
        return qkv_tiles

    def emit_phi(g, qkv_tiles):
        g0 = g * GT
        for wn, dst in (("q", phq_tm), ("k", phk_tm)):
            for ti in range(GT):
                t = g0 + ti
                for o in range(NO):
                    sig, ps, i = qkv_tiles[(wn, ti, o)]
                    ew = [sig]
                    if o in last_e1:
                        ew.append(last_e1[o])
                    S.add("scalar", (lambda e, ps=ps, o=o: e.activation(
                        e1_sb[:, o * 512:(o + 1) * 512], ps[:, :], ACTF.Exp)),
                        waits=tuple(ew), signal=f"e1_{wn}_{t}_{o}")

                    def phi(e, ps=ps, o=o, dst=dst, ti=ti):
                        e.tensor_scalar(r_sb[:, o * 512:(o + 1) * 512],
                                        ps[:, :], 0.0, None, ALU.max)
                        return e.scalar_tensor_tensor(
                            dst[:, ti, o * 512:(o + 1) * 512],
                            e1_sb[:, o * 512:(o + 1) * 512], 1.0,
                            r_sb[:, o * 512:(o + 1) * 512], ALU.min, ALU.add)
                    pw = [f"e1_{wn}_{t}_{o}"]
                    if g >= 1:
                        pw.append(f"phifm_g{g - 1}")
                        if wn == "k":
                            pw += [f"kvps_{g0 - 1}_1", f"wps_{g0 - 1}_1"]
                    S.add("vector", phi, waits=tuple(pw),
                          signal=(f"bigfree_{i}", f"phi_{wn}_{t}_{o}"))
                    last_e1[o] = f"phi_{wn}_{t}_{o}"
        for ti in range(GT):
            t = g0 + ti
            for o in range(NO):
                sig, ps, i = qkv_tiles[("v", ti, o)]
                vw = [sig]
                if t >= 2:
                    vw.append(f"numdone_{t - 2}")
                S.add("scalar", (lambda e, ps=ps, o=o, ti=ti: e.copy(
                    vex[:, ti, 8 * o:8 * o + 8, 0:D], ps[:, :])),
                    waits=tuple(vw), signal=(f"bigfree_{i}", f"vcp_{t}_{o}"))

        S.add("vector", lambda e: e.engine_nop(),
              waits=tuple(f"phi_{wn}_{g0 + ti}_{o}" for wn in ("q", "k")
                          for ti in range(GT) for o in range(NO)),
              signal=f"phitm_g{g}")

    def emit_phifm(g):
        S.add("sync", (lambda e, g=g: [
            e.dma_start_transpose(dfm[:, :, tt * 128:(tt + 1) * 128],
                                  dtm[:, tt, :].rearrange(
                                      "p (c m) -> p c m", c=NK))
            for (dfm, dtm) in ((phq_fm, phq_tm), (phk_fm, phk_tm))
            for tt in range(GT)]),
            waits=(f"phitm_g{g}",) +
                  ((f"attn_done_g{g - 1}",) if g >= 1 else ()),
            signal=f"phifm_g{g}", inc=2 * GT * 16)

    def emit_attn(g):
        g0 = g * GT
        for ti in range(GT):
            t = g0 + ti

            w_sbf = [f"numdone_{t - 1}"] if t >= 1 else ["initS"]
            S.add("vector", (lambda e: e.tensor_copy(S_bf[:, :, :],
                                                     S_sb[:, :, :])),
                  waits=tuple(w_sbf), signal=f"sbf_{t}")

            for w in range(1):
                def kvw(e, ti=ti, w=w):
                    mm = None
                    for j in range(4):
                        hp = 4 * w + j
                        for hh in range(2):
                            h = 2 * hp + hh
                            mm = e.matmul(psK[hh * 64:hh * 64 + 64,
                                              j * 65:(j + 1) * 65],
                                          phk_tm[:, ti, h * D:(h + 1) * D],
                                          vex[:, ti, h, :], start=True,
                                          stop=True,
                                          tile_position=(0, hh * 64))
                    return mm
                wk = [f"phi_k_{t}_0", f"phi_k_{t}_1", f"vcp_{t}_0",
                      f"vcp_{t}_1"]
                if t >= 1 or w >= 1:
                    pt, pw = (t, 0) if w == 1 else (t - 1, 1)
                    wk.append(f"supd_{pt}_{pw}")
                S.add("tensor", kvw, waits=tuple(wk), signal=f"kvps_{t}_{w}")

            for w in range(2):
                def ww(e, ti=ti, w=w):
                    mm = None
                    for j in range(4):
                        hp = 4 * w + j
                        for hh in range(2):
                            h = 2 * hp + hh
                            mm = e.matmul(psW[hh * 64:hh * 64 + 64, j, :],
                                          phk_tm[:, ti, h * D:(h + 1) * D],
                                          mask_sb[:, :], start=True,
                                          stop=True,
                                          tile_position=(0, hh * 64))
                    return mm
                wk = [f"phi_k_{t}_0", f"phi_k_{t}_1"]
                if t >= 1 or w >= 1:
                    pt, pw = (t, 0) if w == 1 else (t - 1, 1)
                    wk.append(f"u_{pt}_{pw}")
                S.add("tensor", ww, waits=tuple(wk), signal=f"wps_{t}_{w}")

            for w in range(2):
                def uf(e, ti=ti, w=w):
                    for j in range(4):
                        hp = 4 * w + j
                        e.scalar_tensor_tensor(
                            u_sb[:, hp, :], psW[:, j, :],
                            S_sb[:, hp, D:D + 1],
                            phq_fm[:, hp, ti * 128:(ti + 1) * 128],
                            ALU.add, ALU.mult)
                    return e.engine_nop()
                wk = [f"wps_{t}_{w}", f"phifm_g{g}", f"sbf_{t}"]
                if t >= 1:
                    wk.append(f"densel_{t - 1}")
                S.add("vector", uf, waits=tuple(wk), signal=f"u_{t}_{w}")

            def supd0(e):
                return e.tensor_tensor(
                    S_sb[:, 0:4, :], S_sb[:, 0:4, :],
                    psK[:, 0:260].rearrange("p (j e) -> p j e", e=65),
                    ALU.add)
            S.add("vector", supd0,
                  waits=(f"kvps_{t}_0", f"u_{t}_0", f"u_{t}_1"),
                  signal=f"supd_{t}_0")

            for w in range(4):
                def atw(e, ti=ti, w=w):
                    ps = psT[w % 2]
                    hh = w % 2
                    mm = None
                    for j in range(4):
                        h = 8 * (w // 2) + 2 * j + hh
                        ce = h // 2
                        mm = e.matmul(ps[:, j, :],
                                      phk_fm[hh * 64:hh * 64 + 64, ce,
                                             ti * 128:(ti + 1) * 128],
                                      phq_fm[hh * 64:hh * 64 + 64, ce,
                                             ti * 128:(ti + 1) * 128],
                                      start=True, stop=True,
                                      tile_position=(hh * 64, 0))
                    return mm
                wk = [f"phifm_g{g}"]
                if t >= 1 or w >= 2:
                    pt, pw = (t, w - 2) if w >= 2 else (t - 1, w + 2)
                    wk.append(f"atm_{pt}_{pw}")
                S.add("tensor", atw, waits=tuple(wk), signal=f"atps_{t}_{w}")

            for w in range(4):
                def atmf(e, w=w):
                    ps = psT[w % 2]
                    return e.scalar_tensor_tensor(
                        atm[:, 4 * w:4 * w + 4, :], ps[:, :, :], 1.0,
                        mask_sb[:, :].rearrange("p (a c) -> p a c", a=1)
                        .broadcast_to([128, 4, 128]), ALU.mult, ALU.mult)
                wk = [f"atps_{t}_{w}"]
                if t >= 1:
                    wk.append(f"numdone_{t - 1}")
                S.add("vector", atmf, waits=tuple(wk), signal=f"atm_{t}_{w}")

            def supd1(e):
                return e.tensor_tensor(
                    S_sb[:, 4:8, :], S_sb[:, 4:8, :],
                    psK[:, 0:260].rearrange("p (j e) -> p j e", e=65),
                    ALU.add)
            S.add("vector", supd1, waits=(f"kvps_{t}_1",),
                  signal=f"supd_{t}_1")

            for w in range(1, 2):
                def kvw1(e, ti=ti, w=w):
                    mm = None
                    for j in range(4):
                        hp = 4 * w + j
                        for hh in range(2):
                            h = 2 * hp + hh
                            mm = e.matmul(psK[hh * 64:hh * 64 + 64,
                                              j * 65:(j + 1) * 65],
                                          phk_tm[:, ti, h * D:(h + 1) * D],
                                          vex[:, ti, h, :], start=True,
                                          stop=True,
                                          tile_position=(0, hh * 64))
                    return mm
                S.add("tensor", kvw1,
                      waits=(f"phi_k_{t}_0", f"phi_k_{t}_1", f"vcp_{t}_0",
                             f"vcp_{t}_1", f"supd_{t}_0"),
                      signal=f"kvps_{t}_1")

            def densel(e):
                mm = None
                for hp in range(8):
                    mm = e.matmul(psD[0:16, 0:128], sel_sb[:, hp, :],
                                  u_sb[:, hp, :], start=(hp == 0),
                                  stop=(hp == 7))
                return mm
            wk = [f"u_{t}_0", f"u_{t}_1"]
            if t >= 1:
                wk.append(f"rdenx_{t - 1}")
            S.add("tensor", densel, waits=tuple(wk), signal=f"densel_{t}")

            S.add("scalar", (lambda e: e.copy(den_f[:, :],
                                              psD[0:16, 0:128])),
                  waits=(f"densel_{t}",) + ((f"denT_{t - 1}",) if t >= 1
                                            else (f"wcvt_{4 * NK * 2 - 1}",)),
                  signal=f"denf_{t}")

            S.add("tensor", (lambda e: e.transpose(
                psD[:, 128:144], den_f[:, :], idn_sb[0:16, 0:16])),
                waits=(f"denf_{t}",), signal=f"denT_{t}")

            def rdenx(e, ti=ti):
                ssync(e, e.tensor_scalar(rden[:, ti, :], psD[:, 128:144],
                                         1e-6, None, ALU.add))
                return e.reciprocal(rden[:, ti, :], rden[:, ti, :])
            S.add("vector", rdenx, waits=(f"denT_{t}",), signal=f"rdenx_{t}")

            for w in range(2):
                def numw(e, ti=ti, w=w):
                    mm = None
                    for j in range(8):
                        h = 8 * w + j
                        ce, hh = h // 2, h % 2
                        slot = 4 * ((h % 2) + 2 * (h // 8)) + (h % 8) // 2
                        e.matmul(psN[:, j, :], atm[:, slot, :],
                                 vex[:, ti, h, 0:D], start=True, stop=False)
                        mm = e.matmul(psN[:, j, :],
                                      phq_fm[hh * 64:hh * 64 + 64, ce,
                                             ti * 128:(ti + 1) * 128],
                                      S_bf[hh * 64:hh * 64 + 64, h // 2,
                                           0:D],
                                      start=False, stop=True,
                                      tile_position=(hh * 64, 0))
                    return mm
                wk = [f"atm_{t}_{2 * w}", f"atm_{t}_{2 * w + 1}",
                      f"sbf_{t}", f"vcp_{t}_0", f"vcp_{t}_1",
                      f"phifm_g{g}"]
                if t >= 1 or w >= 1:
                    pt, pw = (t, 0) if w == 1 else (t - 1, 1)
                    wk.append(f"ynum_{pt}_{pw}")
                S.add("tensor", numw, waits=tuple(wk), signal=f"nps_{t}_{w}")
            S.add("tensor", lambda e: e.nop(),
                  waits=(f"nps_{t}_0", f"nps_{t}_1"), signal=f"numdone_{t}")

            for w in range(2):
                def ynum(e, ti=ti, w=w):
                    return e.tensor_tensor(
                        y_tm[:, ti, w * 512:(w + 1) * 512].rearrange(
                            "p (h d) -> p h d", d=D),
                        psN[:, :, :],
                        rden[:, ti, 8 * w:8 * w + 8].rearrange(
                            "p (h o) -> p h o", o=1).broadcast_to(
                                [128, 8, D]), ALU.mult)
                yw = [f"nps_{t}_{w}", f"rdenx_{t}"]
                if g >= 1:
                    yw.append(f"yfm_g{g - 1}")
                S.add("vector", ynum, waits=tuple(yw),
                      signal=f"ynum_{t}_{w}")

        S.add("vector", lambda e: e.engine_nop(),
              waits=(f"numdone_{g0 + GT - 1}",), signal=f"attn_done_g{g}")
        S.add("vector", lambda e: e.engine_nop(),
              waits=(f"ynum_{g0}_1", f"ynum_{g0 + 1}_1"),
              signal=f"ytm_g{g}")

    def emit_yfm(g):
        S.add("sync", (lambda e, g=g: [
            e.dma_start_transpose(y_fm[:, :, tt * 128:(tt + 1) * 128],
                                  y_tm[:, tt, :].rearrange(
                                      "p (c m) -> p c m", c=NK))
            for tt in range(GT)]),
            waits=(f"ytm_g{g}",) +
                  ((f"p_done_g{g - 1}",) if g >= 1 else ()),
            signal=f"yfm_g{g}", inc=GT * 16)

    def emit_P(g):
        g0 = g * GT
        p_tiles = []
        for ti in range(GT):
            t = g0 + ti
            for o in range(NO):
                sig = f"pps_{t}_{o}"
                ps, i = big_tile(
                    sig,
                    (lambda k, ti=ti: y_fm[:, k, ti * 128:(ti + 1) * 128]),
                    (lambda k, o=o: w_sb[3][:, k, o * 512:(o + 1) * 512]),
                    waits=(f"yfm_g{g}", f"wcvt_{(3 * NK + NK) * 2 - 1}",
                           "init_v"),
                    outer=(lambda o=o: (ones_sb[0:1, :],
                                        bp_sb[0:1, o * 512:(o + 1) * 512])))
                p_tiles.append((t, sig, ps, i, o))
        S.add("tensor", lambda e: e.nop(),
              waits=(f"pps_{g0 + GT - 1}_{NO - 1}",),
              signal=f"p_done_g{g}")

        for t, sig, ps, i, o in p_tiles:
            def pev(e, ps=ps, t=t, o=o):
                ssync(e, e.scalar_tensor_tensor(
                    x_sb[:, t % RT, o * 512:(o + 1) * 512], ps[:, :], 0.0,
                    x_sb[:, t % RT, o * 512:(o + 1) * 512], ALU.add, ALU.add,
                    accum_out=tA[:, t:t + 1]))
                if o == 0:
                    return e.tensor_copy(sxC[:, t:t + 1], tA[:, t:t + 1])
                return e.tensor_tensor(sxC[:, t:t + 1], sxC[:, t:t + 1],
                                       tA[:, t:t + 1], ALU.add)
            S.add("vector", pev, waits=(sig,),
                  signal=(f"bigfree_{i}", f"pev_{t}_{o}"))

    def emit_store(g):
        s0 = (g * GT) % RT
        S.add("sync", (lambda e, g=g, s0=s0: [
            e.dma_start(out=out_h.rearrange("p (t e) -> p t e", e=E)[
                :, g * GT:(g + 1) * GT, :],
                in_=x_sb[:, s0:s0 + GT, :])]),
            waits=tuple(f"ffnev_f2_{g * GT + ti}_{o}"
                        for ti in range(GT) for o in range(NO)),
            signal=f"store_{g}", inc=16)

    def emit_w(lo, hi):
        for li, wts, fn in wl_list[lo:hi]:
            S.add("sync", fn, waits=tuple(wts), signal=f"wload_{li}", inc=16)
        for li, fn in wc_list[lo:hi]:
            S.add("scalar", fn, waits=(f"wload_{li}",), signal=f"wcvt_{li}")

    emit_w(0, 48)
    t_f1 = {}
    emit_sums(0)
    emit_sums(1)
    emit_quant("f1", 0, sxA)
    t_f1[0] = emit_fT_mm("f1", 0)
    emit_quant("f1", 1, sxA)
    for t, tl in t_f1[0]:
        ffn_evac(t, "f1", tl, sxB)
    t_f1[1] = emit_fT_mm("f1", 1)
    emit_ln2(0)
    for t, tl in t_f1[1]:
        ffn_evac(t, "f1", tl, sxB)
    emit_ln2fm(0)
    qk0 = emit_qkv(0)
    emit_phi(0, qk0)
    emit_w(48, 64)
    S.add("scalar", (lambda e: e.activation(S_sb[:, :, :], S_sb[:, :, :],
                                            ACTF.Copy, scale=0.0, bias=0.0)),
          waits=(f"wcvt_{4 * NK * 2 - 1}",), signal="initS")
    emit_phifm(0)

    t_f2 = {}
    for g in range(NG):
        if g >= 1 and g + 1 < NG:
            for t, tl in t_f1[g + 1]:
                ffn_evac(t, "f1", tl, sxB)
        emit_attn(g)
        if g >= 1:
            t_f2[g - 1] = emit_fmm("f2", g - 1)
        if g >= 1:
            for t, tl in t_f2[g - 1]:
                ffn_evac(t, "f2", tl, None)
            emit_store(g - 1)
            gl = g - 1 + RT // GT
            if gl < NG:
                emit_xload(gl, waits=(f"store_{g - 1}",))
        if g + 1 < NG:
            emit_ln2(g + 1)
            emit_ln2fm(g + 1)
        if g + 2 < NG:
            emit_sums(g + 2)
            emit_quant("f1", g + 2, sxA)
            emit_fT_cast("f1", g + 2)
        if g + 1 < NG:
            qk = emit_qkv(g + 1)
            emit_phi(g + 1, qk)
        emit_yfm(g)
        emit_P(g)
        if g + 2 < NG:
            t_f1[g + 2] = emit_fmm("f1", g + 2)
        emit_quant("f2", g, sxC)
        emit_fT_cast("f2", g)
        if g + 1 < NG:
            emit_phifm(g + 1)

    t_f2[NG - 1] = emit_fmm("f2", NG - 1)
    for t, tl in t_f2[NG - 1]:
        ffn_evac(t, "f2", tl, None)
    emit_store(NG - 1)

    S.add("sync", lambda e: e.nop(),
          waits=tuple(f"store_{g}" for g in range(NG)), signal=None)

    S.resolve()
    sems = {name: nc.alloc_semaphore(f"sem_{name}")
            for name in ENGS + ("gpsimd_dma",)}
    with nc.Block() as block:
        S.emit(block, sems)
    return nc


_NC_CACHE = {}


def _get_nc():
    if "nc" not in _NC_CACHE:
        _NC_CACHE["nc"] = build_program()
    return _NC_CACHE["nc"]


def _consts():
    s, c = np.meshgrid(np.arange(128), np.arange(128), indexing="ij")
    mask = (s <= c).astype(ml_dtypes.bfloat16)
    idn = np.eye(128, dtype=np.float32)
    sel = np.zeros((128, 8, 16), dtype=ml_dtypes.bfloat16)
    for p in range(128):
        for hp in range(8):
            sel[p, hp, 2 * hp + (p // 64)] = 1.0
    ones = np.ones((1, 128), dtype=ml_dtypes.bfloat16)
    cp = np.zeros((128, 4), dtype=np.float32)
    cp[:, 0] = -1.0
    cp[:, 1] = -0.5
    cp[:, 2] = 80.5
    cp[:, 3] = -80.0
    return mask, idn, sel, ones, cp


def _layout_w(wT):
    return np.ascontiguousarray(
        wT.reshape(NK, 128, 2, 512).transpose(0, 2, 1, 3).reshape(2 * E,
                                                                  E // 2))


def _layout_x(x):
    return np.ascontiguousarray(
        x.reshape(NT, 128, E).transpose(1, 0, 2).reshape(128, NT * E))


def _unlayout_out(o):
    return np.ascontiguousarray(
        o.reshape(128, NT, E).transpose(1, 0, 2).reshape(T, E))


def kernel(L0_attn_x, L0_attn_Wq, L0_attn_Wk, L0_attn_Wv, L0_attn_Wp,
           L0_attn_bp, L0_ffn_weight, L0_ffn_weight_scale, L0_ffn_bias):
    nc = _get_nc()
    mask, idn, sel, ones, cp = _consts()
    x = np.ascontiguousarray(np.asarray(L0_attn_x, dtype=np.float32))
    wqT = _layout_w(np.asarray(L0_attn_Wq, np.float32).T)
    wkT = _layout_w(np.asarray(L0_attn_Wk, np.float32).T)
    wvT = _layout_w(np.asarray(L0_attn_Wv, np.float32).T)
    wpT = _layout_w(np.asarray(L0_attn_Wp, np.float32).T)
    w8 = np.asarray(L0_ffn_weight)
    w8T = np.ascontiguousarray(w8.view(np.uint8).T)
    bp = np.ascontiguousarray(np.asarray(L0_attn_bp, np.float32).reshape(1, E))
    fb = np.ascontiguousarray(np.asarray(L0_ffn_bias, np.float32).reshape(1, E))
    ws = np.full((128, 1), np.float32(np.asarray(L0_ffn_weight_scale)),
                 dtype=np.float32)

    shared = {"wqT": wqT, "wkT": wkT, "wvT": wvT, "wpT": wpT, "w8T": w8T,
              "bp": bp, "fb": fb, "ws": ws, "cp": cp, "maskT": mask,
              "idn": idn, "sel": sel, "ones": ones}
    in_maps = [dict(shared, x=_layout_x(x[b])) for b in range(8)]
    res = run_bass_kernel_spmd(nc, in_maps, core_ids=list(range(8)))
    out = np.stack([_unlayout_out(np.asarray(r["out"])) for r in res.results],
                   axis=0)
    return out.astype(np.float32)


# version 12
# speedup vs baseline: 1.0450x; 1.0450x over previous
import numpy as np
import ml_dtypes
import concourse.bass as bass
import concourse.mybir as mybir
from concourse.bass_utils import run_bass_kernel_spmd

F32 = mybir.dt.float32
I32 = mybir.dt.int32
BF16 = mybir.dt.bfloat16
U8 = mybir.dt.uint8
FP8 = mybir.dt.float8e4
ALU = mybir.AluOpType
ACTF = mybir.ActivationFunctionType
AXX = mybir.AxisListType.X

T = 2048
E = 1024
H = 16
D = 64
NT = T // 128
GT = 2
NG = NT // GT
NK = E // 128
NO = 2
RT = 10

ENGS = ("sync", "tensor", "vector", "scalar", "gpsimd")
PHASE = 4


class Sched:
    def __init__(self):
        self.items = []
        self.M = {}

    def add(self, eng, fn, waits=(), signal=None, inc=1, sem=None):
        self.items.append((eng, tuple(waits), signal, inc, fn, sem or eng))

    def resolve(self):
        val = {}
        for eng, waits, signal, inc, fn, sk in self.items:
            if signal is not None:
                val[sk] = val.get(sk, 0) + inc
                names = signal if isinstance(signal, tuple) else (signal,)
                for nm in names:
                    assert nm not in self.M, f"dup signal {nm}"
                    self.M[nm] = (sk, val[sk])
        for eng, waits, signal, inc, fn, sk in self.items:
            for w in waits:
                assert w in self.M, f"unresolved wait {w} (engine {eng})"

    def emit(self, block, sems):
        per_eng = {e: [] for e in ENGS}
        for it in self.items:
            per_eng[it[0]].append(it)

        def make_runner(eng_name):
            def run(e):
                for _, waits, signal, inc, fn, sk in per_eng[eng_name]:
                    for w in waits:
                        psk, v = self.M[w]
                        e.wait_ge(sems[psk], v)
                    insts = fn(e)
                    if signal is not None:
                        if isinstance(insts, list):
                            for inst in insts:
                                inst.then_inc(sems[sk], 16)
                        else:
                            insts.then_inc(sems[sk], 1)
            return run

        for name in ENGS:
            if per_eng[name]:
                getattr(block, name)(make_runner(name))


def build_program():
    nc = bass.Bass("TRN2", target_bir_lowering=False, debug=False,
                   detect_race_conditions=False)

    x_h = nc.dram_tensor("x", [128, NT * E], F32, kind="ExternalInput").ap()
    wT_h = [nc.dram_tensor(n, [2 * E, E // 2], F32, kind="ExternalInput").ap()
            for n in ("wqT", "wkT", "wvT", "wpT")]
    w8T_h = nc.dram_tensor("w8T", [E, E], U8, kind="ExternalInput").ap()
    bp_h = nc.dram_tensor("bp", [1, E], F32, kind="ExternalInput").ap()
    fb_h = nc.dram_tensor("fb", [1, E], F32, kind="ExternalInput").ap()
    ws_h = nc.dram_tensor("ws", [128, 1], F32, kind="ExternalInput").ap()
    cp_h = nc.dram_tensor("cp", [128, 4], F32, kind="ExternalInput").ap()
    mask_h = nc.dram_tensor("maskT", [128, 128], BF16, kind="ExternalInput").ap()
    idn_h = nc.dram_tensor("idn", [128, 128], F32, kind="ExternalInput").ap()
    sel_h = nc.dram_tensor("sel", [128, 8, 16], BF16, kind="ExternalInput").ap()
    ones_h = nc.dram_tensor("ones", [1, 128], BF16, kind="ExternalInput").ap()
    out_h = nc.dram_tensor("out", [128, NT * E], F32,
                           kind="ExternalOutput").ap()

    A = nc.alloc_sbuf_tensor
    x_sb = A("x_sb", [128, RT, E], F32)
    w_sb = [A(f"w{n}_sb", [128, NK, E], BF16) for n in "qkvp"]
    w8_sb = A("w8_sb", [128, NK, E], FP8)
    w8u = w8_sb.ap().bitcast(U8)
    qf8a = A("qf8a", [128, NK, GT * 128], FP8)
    qf8b = A("qf8b", [128, NK, GT * 128], FP8)
    qf8w = {"f1": qf8a, "f2": qf8b}
    ralpha = A("ralpha", [128, NT], BF16)
    iarb = A("iarb", [1, 2, GT * 128], BF16)
    stgS = A("stgS", [128, 544], F32)
    stg2 = A("stg2", [128, 512], F32)
    stg = [stgS[:, 0:512], stg2[:, :]]

    ws_sb = A("ws_sb", [128, 1], F32)
    cp_sb = A("cp_sb", [128, 4], F32)
    mask_sb = A("mask_sb", [128, 128], BF16)
    idn_sb = A("idn_sb", [128, 128], F32)
    sel_sb = A("sel_sb", [128, 8, 16], BF16)
    ones_sb = A("ones_sb", [1, 128], BF16)
    bp_sb = A("bp_sb", [1, E], BF16)
    fbh_sb = A("fbh_sb", [1, E], BF16)

    sxA = A("sxA", [128, NT], F32)
    sxB = A("sxB", [128, NT], F32)
    sxC = A("sxC", [128, NT], F32)
    mu_c = A("mu_c", [128, NT], F32)
    ssq_c = A("ssq_c", [128, NT], F32)
    mxa = A("mxa", [128, NT], F32)
    mxb = A("mxb", [128, NT], F32)
    rstd_c = A("rstd_c", [128, NT], F32)
    amax_c = A("amax_c", [128, NT], F32)
    rq_c = A("rq_c", [128, NT], F32)
    alpha_c = A("alpha_c", [128, NT], F32)
    tA = A("tA", [128, NT], F32)
    tB = A("tB", [128, NT], F32)

    q_dev = A("q_dev", [128, E], F32)
    q_gh = A("q_gh", [128, E], I32)
    q_t7 = A("q_t7", [128, E], F32)
    q_ub = A("q_ub", [128, E], I32)
    q_a1 = A("q_a1", [128, E], F32)
    q_a2 = A("q_a2", [128, E], BF16)
    sq_scr = q_a2

    qtm = A("qtm", [128, GT, E], BF16)
    fm_sh = A("fm_sh", [128, NK, GT * 128], BF16)
    qfm = fm_sh
    ln2fm = A("ln2fm_sb", [128, NK, GT * 128], BF16)
    y_fm = A("y_fm_sb", [128, NK, GT * 128], BF16)
    phq_fm = A("phq_fm_sb", [128, NK, GT * 128], BF16)

    ln2tm = A("ln2tm", [128, GT, E], BF16)
    y_tm = A("y_tm_sb", [128, GT, E], BF16)
    e1_sb = ln2tm[:, 0, :]
    r_sb = ln2tm[:, 1, :]
    ph_tm2 = A("ph_tm2", [128, 2, GT, E], BF16)
    phq_tm = ph_tm2[:, 0]
    phk_tm = ph_tm2[:, 1]
    phk_fm = A("phk_fm", [128, NK, GT * 128], BF16)
    vex = A("vex", [128, 2, H, D + 1], BF16)
    atm = A("atm_sb", [128, 16, 128], BF16)
    u_sb = A("u_sb", [128, 8, 128], BF16)
    S_sb = stgS[:, 0:520].rearrange("p (a b) -> p a b", b=D + 1)
    S_bf = A("S_bf", [128, 8, D + 1], BF16)
    den_f = stg2[0:16, 0:128]
    rden = A("rden", [128, GT, 16], F32)

    d_i = q_ub
    d_e = q_gh
    d_f = q_t7[:, :].bitcast(I32)
    d_s = q_dev
    d_t1 = q_a1
    d_t2 = q_a2
    d_p2 = q_ub

    psA = nc.alloc_psum_tensor("psA", [128, 512], F32)
    psB = nc.alloc_psum_tensor("psB", [128, 512], F32)
    psK = nc.alloc_psum_tensor("psK", [128, 512], F32)
    psT1 = nc.alloc_psum_tensor("psT1", [128, 4, 128], F32)
    psT2 = nc.alloc_psum_tensor("psT2", [128, 4, 128], F32)
    psW = nc.alloc_psum_tensor("psW", [128, 4, 128], F32)
    psN = nc.alloc_psum_tensor("psN", [128, 8, D], F32)
    psD = nc.alloc_psum_tensor("psD", [128, 512], F32)
    psAB = [psA, psB]
    psT = [psT1, psT2]

    S = Sched()

    ss_sem = {}
    ss_val = {}

    def ssync(e, inst):
        nm = str(e.engine)
        if nm not in ss_sem:
            ss_sem[nm] = nc.alloc_semaphore(f"ss_{len(ss_sem)}")
            ss_val[nm] = 0
        ss_val[nm] += 1
        inst.then_inc(ss_sem[nm], 1)
        e.wait_ge(ss_sem[nm], ss_val[nm])
        return inst

    S.add("sync", lambda e: [
        e.dma_start(out=ws_sb[:, :], in_=ws_h),
        e.dma_start(out=cp_sb[:, :], in_=cp_h),
        e.dma_start(out=mask_sb[:, :], in_=mask_h),
        e.dma_start(out=idn_sb[:, :], in_=idn_h),
        e.dma_start(out=sel_sb[:, :, :], in_=sel_h),
        e.dma_start(out=ones_sb[:, :], in_=ones_h),
        e.dma_start(out=q_dev[0:1, 0:E], in_=bp_h),
        e.dma_start(out=q_t7[0:1, 0:E], in_=fb_h),
    ], signal="smallc", inc=8 * 16)

    def emit_xload(g, waits=()):
        s0 = (g * GT) % RT
        S.add("sync", (lambda e, g=g, s0=s0: [
            e.dma_start(out=x_sb[:, s0:s0 + GT, :],
                        in_=x_h.rearrange("p (t e) -> p t e", e=E)[
                            :, g * GT:(g + 1) * GT, :])]),
            waits=waits, signal=f"x_in_g{g}", inc=16)

    sums_done = set()

    def emit_sums(g):
        if g in sums_done:
            return
        sums_done.add(g)

        def sums_g(e, g=g):
            for ti in range(GT):
                t = g * GT + ti
                e.reduce_sum(sxA[:, t:t + 1], x_sb[:, t % RT, :], AXX)
            return e.engine_nop()
        S.add("vector", sums_g, waits=(f"x_in_g{g}",), signal=f"sx1_g{g}")

    for g in range(min(NG, 2)):
        emit_xload(g)
    S.add("sync", lambda e: [
        e.dma_start(out=w8u[:, :, :],
                    in_=w8T_h.rearrange("(c p) o -> p c o", p=128)),
    ], waits=("smallc",), signal="consts", inc=16)
    for g in range(2, min(NG, RT // GT)):
        emit_xload(g)

    def init_v(e):
        e.tensor_copy(bp_sb[0:1, :], q_dev[0:1, 0:E])
        e.tensor_scalar(fbh_sb[0:1, :], q_t7[0:1, 0:E], 0.5, None, ALU.mult)
        return e.memset(vex[:, :, :, D:D + 1], 1.0)
    S.add("vector", init_v, waits=("smallc",), signal="init_v")

    def make_dec(c):
        def dec(e):
            e.tensor_copy(d_i[:, :], w8u[:, c, :])
            e.tensor_scalar(d_e[:, :], d_i[:, :], 0x78, 3, ALU.bitwise_and,
                            ALU.logical_shift_right)
            e.tensor_scalar(d_f[:, :], d_i[:, :], 0x7, None, ALU.bitwise_and)
            e.tensor_scalar(d_p2[:, :], d_i[:, :], 7, None,
                            ALU.logical_shift_right)
            e.tensor_scalar(d_s[:, :], d_p2[:, :], -1.0, 0.5, ALU.mult,
                            ALU.add)
            e.tensor_scalar(d_t1[:, :], d_e[:, :], 1.0, 8.0, ALU.min,
                            ALU.mult)
            e.tensor_tensor(d_t1[:, :], d_t1[:, :], d_f[:, :], ALU.add)
            e.tensor_scalar(d_t2[:, :], d_e[:, :], 1.0, 117.0, ALU.max,
                            ALU.add)
            e.tensor_scalar(d_p2[:, :], d_t2[:, :], 8388608.0, None, ALU.mult)
            e.tensor_tensor(d_t1[:, :], d_t1[:, :], d_p2[:, :].bitcast(F32),
                            ALU.mult)
            return e.tensor_tensor(w8_sb[:, c, :], d_t1[:, :], d_s[:, :],
                                   ALU.mult)
        return dec

    wload_items = []
    wcvt_items = []
    for wi in range(4):
        for c in range(NK):
            for hf in range(2):
                li = (wi * NK + c) * 2 + hf
                wts = ["init_v"] if li < 2 else [f"wcvt_{li - 2}"]
                wload_items.append((li, wts, (lambda e, wi=wi, c=c, hf=hf,
                                              li=li: [
                    e.dma_start(out=stg[li % 2][:, :],
                                in_=wT_h[wi][(c * 2 + hf) * 128:
                                             (c * 2 + hf + 1) * 128, :])])))
                wcvt_items.append((li, (lambda e, wi=wi, c=c, hf=hf, li=li:
                                        e.copy(
                    w_sb[wi][:, c, hf * 512:(hf + 1) * 512],
                    stg[li % 2][:, :]))))

    big_i = [0]

    def big_tile(sig, lhs_fn, rhs_fn, waits=(), outer=None):
        i = big_i[0]
        big_i[0] += 1
        ps = psAB[i % 2]
        w = list(waits)
        if i >= 2:
            w.append(f"bigfree_{i - 2}")

        def fn(e, ps=ps):
            mm = None
            for k in range(NK):
                mm = e.matmul(ps[:, :], lhs_fn(k), rhs_fn(k), start=(k == 0),
                              stop=(k == NK - 1 and outer is None))
            if outer is not None:
                lhs, rhs = outer()
                mm = e.matmul(ps[:, :], lhs, rhs, start=False, stop=True)
            return mm
        S.add("tensor", fn, waits=w, signal=sig)
        return ps, i

    prevq = {"sig": None}

    def ffn_quant_tile(t, which, sx_src, extra_wait=()):
        ti = t % GT
        tg = f"{which}_{t}"

        def stats1(e):
            ssync(e, e.tensor_scalar(mu_c[:, t:t + 1], sx_src[:, t:t + 1],
                                     1.0 / E, None, ALU.mult))
            e.tensor_scalar(q_dev[:, :], x_sb[:, t % RT, :],
                            mu_c[:, t:t + 1],
                            None, ALU.subtract, ALU.min,
                            accum_out=mxa[:, t:t + 1])
            return e.tensor_scalar(q_dev[:, :], x_sb[:, t % RT, :],
                                   mu_c[:, t:t + 1], None, ALU.subtract,
                                   ALU.max, accum_out=mxb[:, t:t + 1])
        S.add("vector", stats1, waits=tuple(extra_wait), signal=f"dev_{tg}")

        S.add("scalar", (lambda e: e.activation(
            sq_scr[:, :], q_dev[:, :], ACTF.Square,
            accum_out=ssq_c[:, t:t + 1])), waits=(f"dev_{tg}",),
            signal=f"ssq_{tg}")

        def stats2(e):
            ssync(e, e.scalar_tensor_tensor(amax_c[:, t:t + 1],
                                            mxa[:, t:t + 1], -1.0,
                                            mxb[:, t:t + 1], ALU.mult,
                                            ALU.max))
            ssync(e, e.tensor_scalar(tA[:, t:t + 1], amax_c[:, t:t + 1],
                                     1e-30, None, ALU.max))
            ssync(e, e.reciprocal(rq_c[:, t:t + 1], tA[:, t:t + 1]))
            ssync(e, e.tensor_scalar(rq_c[:, t:t + 1], rq_c[:, t:t + 1],
                                     448.0, None, ALU.mult))
            e.tensor_scalar(tB[:, t:t + 1], ssq_c[:, t:t + 1],
                            1.0 / E, 1e-5, ALU.mult, ALU.add)
            return e.tensor_scalar(q_dev[:, :], q_dev[:, :],
                                   rq_c[:, t:t + 1], None, ALU.mult)
        S.add("vector", stats2, waits=(f"ssq_{tg}",),
              signal=(f"st2_{tg}", f"y_{tg}"))

        def rstd_f(e):
            ssync(e, e.activation(tB[:, t:t + 1], tB[:, t:t + 1], ACTF.Ln))
            return e.activation(rstd_c[:, t:t + 1], tB[:, t:t + 1], ACTF.Exp,
                                bias=0.0, scale=cp_sb[:, 1:2])
        S.add("scalar", rstd_f, waits=(f"st2_{tg}",), signal=f"rstd_{tg}")

        def expo(e):
            e.tensor_scalar(q_gh[:, :], q_dev[:, :].bitcast(I32), 0x7F800000,
                            None, ALU.bitwise_and)
            e.tensor_scalar(q_gh[:, :], q_gh[:, :], 33554432.0, 981467136.0,
                            ALU.subtract, ALU.max)
            e.tensor_scalar(q_a1[:, :].bitcast(I32), q_gh[:, :], -1.0,
                            2122317824.0, ALU.mult, ALU.add)
            e.tensor_tensor(q_t7[:, :], q_dev[:, :], q_a1[:, :], ALU.mult)
            e.tensor_scalar(q_t7[:, :], q_t7[:, :], 80.5, None, ALU.add)
            e.tensor_scalar(q_ub[:, :], q_t7[:, :].bitcast(I32), -131072,
                            None, ALU.bitwise_and)
            return e.tensor_tensor(q_dev[:, :], q_t7[:, :],
                                   q_ub[:, :].bitcast(F32), ALU.subtract)
        S.add("vector", expo, waits=(f"y_{tg}",), signal=f"dd_{tg}")

        def signs(e):
            e.activation(q_a1[:, :], q_dev[:, :], ACTF.Sign)
            return e.activation(q_a2[:, :], q_dev[:, :], ACTF.Sign,
                                bias=cp_sb[:, 0:1], scale=1.0)
        S.add("scalar", signs, waits=(f"dd_{tg}",), signal=f"sgn_{tg}")

        def tail2(e):
            ssync(e, e.tensor_tensor(tA[:, t:t + 1], rstd_c[:, t:t + 1],
                                     amax_c[:, t:t + 1], ALU.mult))
            ssync(e, e.tensor_scalar(tA[:, t:t + 1], tA[:, t:t + 1], 1e-12,
                                     2.0 / 448.0, ALU.max, ALU.mult))
            ssync(e, e.tensor_tensor(alpha_c[:, t:t + 1], tA[:, t:t + 1],
                                     ws_sb[:, 0:1], ALU.mult))
            with nc.allow_low_precision(reason="1/alpha is a bf16 matmul "
                                         "operand (bias outer product)"):
                e.reciprocal(ralpha[:, t:t + 1], alpha_c[:, t:t + 1])
            e.tensor_scalar(q_t7[:, :], q_ub[:, :].bitcast(F32), -80.0,
                            None, ALU.add)
            e.tensor_tensor(q_a1[:, :], q_a1[:, :], q_a2[:, :], ALU.add)
            e.scalar_tensor_tensor(q_t7[:, :], q_a1[:, :], 0.5,
                                   q_t7[:, :], ALU.mult, ALU.add)
            return e.tensor_tensor(qtm[:, ti, :], q_t7[:, :],
                                   q_gh[:, :].bitcast(F32), ALU.mult)
        S.add("vector", tail2, waits=(f"sgn_{tg}", f"rstd_{tg}"),
              signal=(f"tl_{tg}", f"qtm_{tg}"))

    def ffn_mm(t, which, g):
        ti = t % GT
        par = ffn_seq_par[which, g]
        tiles = []
        for o in range(NO):
            sig = f"ffnps_{which}_{t}_{o}"
            ps, i = big_tile(
                sig,
                (lambda k, ti=ti, q8=qf8w[which]:
                 q8[:, k, ti * 128:(ti + 1) * 128]),
                (lambda k, o=o: w8_sb[:, k, o * 512:(o + 1) * 512]),
                waits=(f"qf8c_{which}_g{g}", f"iarow_{which}_g{g}",
                       f"wdec_{NK - 1}", "init_v"),
                outer=(lambda o=o, ti=ti, par=par: (
                    iarb[0:1, par, ti * 128:(ti + 1) * 128],
                    fbh_sb[0:1, o * 512:(o + 1) * 512])))
            tiles.append((sig, ps, i, o))
        return tiles

    ffn_seq = [0]
    ffn_seq_par = {}

    def ffn_qcast(which, g, qf8_free_prev):
        si = ffn_seq[0]
        ffn_seq[0] += 1
        par = si % 2
        ffn_seq_par[which, g] = par
        g0 = g * GT
        S.add("scalar", (lambda e, q8=qf8w[which]: e.copy(
            q8[:, :, :], qfm[:, :, :])),
              waits=(f"qfm_{which}_g{g}",) + qf8_free_prev,
              signal=f"qf8c_{which}_g{g}")

        def iarow(e, g0=g0, par=par):
            return [e.dma_start(out=iarb[0:1, par, ti * 128:(ti + 1) * 128],
                                in_=ralpha[:, g0 + ti:g0 + ti + 1])
                    for ti in range(GT)]
        S.add("gpsimd", iarow,
              waits=tuple(f"qtm_{which}_{g0 + ti}" for ti in range(GT)) +
                    qf8_free_prev,
              signal=f"iarow_{which}_g{g}", inc=GT * 16, sem="gpsimd_dma")

    def ffn_evac(t, which, tiles, sx_next):
        tg = f"{which}_{t}"
        for sig, ps, i, o in tiles:
            def ev(e, ps=ps, o=o, t=t):
                ssync(e, e.scalar_tensor_tensor(
                    x_sb[:, t % RT, o * 512:(o + 1) * 512], ps[:, :],
                    alpha_c[:, t:t + 1],
                    x_sb[:, t % RT, o * 512:(o + 1) * 512], ALU.mult,
                    ALU.add, accum_out=tA[:, t:t + 1]))
                if sx_next is not None:
                    if o == 0:
                        return e.tensor_copy(sx_next[:, t:t + 1],
                                             tA[:, t:t + 1])
                    return e.tensor_tensor(sx_next[:, t:t + 1],
                                           sx_next[:, t:t + 1],
                                           tA[:, t:t + 1], ALU.add)
                return e.tensor_copy(tB[:, t:t + 1], tA[:, t:t + 1])
            S.add("vector", ev, waits=(sig,),
                  signal=(f"bigfree_{i}", f"ffnev_{tg}_{o}"))

    wc_list = list(wcvt_items)
    wl_list = list(wload_items)
    dec_iter = iter(range(NK))
    seqst = {"qtm": None, "qfm": None, "mm": None,
             "mm_f1": None, "mm_f2": None}

    def emit_quant(which, g, sx_src):
        g0 = g * GT
        for ti in range(GT):
            t = g0 + ti
            if which == "f1":
                ew = [f"sx1_g{g}"]
            else:
                ew = [f"pev_{t}_{NO - 1}", f"ssq2_{t}"]
            if seqst["qtm"] is not None:
                ew.append(seqst["qtm"])
            ffn_quant_tile(t, which, sx_src, extra_wait=tuple(ew))
            if which == "f1":
                for _ in range(2):
                    try:
                        c = next(dec_iter)
                    except StopIteration:
                        break
                    S.add("vector", make_dec(c),
                          waits=("consts",) if c == 0 else (),
                          signal=f"wdec_{c}")
        seqst["qtm"] = f"qfm_{which}_g{g}"

    def emit_fT_cast(which, g):
        tw = [f"qtm_{which}_{g * GT}", f"qtm_{which}_{g * GT + 1}"]
        if seqst["qfm"] is not None:
            tw.append(seqst["qfm"])
        S.add("sync", (lambda e, g=g: [
            e.dma_start_transpose(qfm[:, :, tt * 128:(tt + 1) * 128],
                                  qtm[:, tt, :].rearrange(
                                      "p (c m) -> p c m", c=NK))
            for tt in range(GT)]),
            waits=tuple(tw), signal=f"qfm_{which}_g{g}", inc=GT * 16)
        seqst["qfm"] = f"qf8c_{which}_g{g}"
        mmprev = seqst["mm_" + which]
        ffn_qcast(which, g,
                  qf8_free_prev=(mmprev,) if mmprev else ("consts",))

    def emit_fmm(which, g):
        tiles = [(g * GT + ti, ffn_mm(g * GT + ti, which, g))
                 for ti in range(GT)]
        S.add("tensor", lambda e: e.nop(),
              waits=(f"ffnps_{which}_{g * GT + GT - 1}_{NO - 1}",),
              signal=f"ffn_mm_done_{which}_g{g}")
        seqst["mm_" + which] = f"ffn_mm_done_{which}_g{g}"
        return tiles

    def emit_fT_mm(which, g):
        emit_fT_cast(which, g)
        return emit_fmm(which, g)

    last_e1 = {}

    def emit_ln2(g):
        g0 = g * GT
        for ti in range(GT):
            t = g0 + ti

            def dev2(e, t=t):
                ssync(e, e.tensor_scalar(mu_c[:, t:t + 1], sxB[:, t:t + 1],
                                         1.0 / E, None, ALU.mult))
                return e.tensor_scalar(q_dev[:, :], x_sb[:, t % RT, :],
                                       mu_c[:, t:t + 1], None, ALU.subtract)
            S.add("vector", dev2, waits=(f"ffnev_f1_{t}_{NO - 1}",),
                  signal=f"dev2_{t}")
            S.add("scalar", (lambda e, t=t: e.activation(
                sq_scr[:, :], q_dev[:, :], ACTF.Square,
                accum_out=ssq_c[:, t:t + 1])), waits=(f"dev2_{t}",),
                signal=f"ssq2_{t}")

            def var2(e, t=t):
                return e.tensor_scalar(tB[:, t:t + 1], ssq_c[:, t:t + 1],
                                       1.0 / E, 1e-5, ALU.mult, ALU.add)
            S.add("vector", var2, waits=(f"ssq2_{t}",), signal=f"var2_{t}")

            def rstd2(e, t=t):
                ssync(e, e.activation(tB[:, t:t + 1], tB[:, t:t + 1],
                                      ACTF.Ln))
                return e.activation(rstd_c[:, t:t + 1], tB[:, t:t + 1],
                                    ACTF.Exp, bias=0.0, scale=cp_sb[:, 1:2])
            S.add("scalar", rstd2, waits=(f"var2_{t}",), signal=f"rstd2_{t}")

            def mkln2(e, t=t, ti=ti):
                return e.tensor_scalar(ln2tm[:, ti, :], q_dev[:, :],
                                       rstd_c[:, t:t + 1], None, ALU.mult)
            lnw = [f"rstd2_{t}"]
            if g >= 1:
                lnw.append(f"ln2fm_g{g - 1}")
            S.add("vector", mkln2, waits=tuple(lnw), signal=f"ln2_{t}")

    def emit_ln2fm(g):
        g0 = g * GT
        S.add("sync", (lambda e, g=g: [
            e.dma_start_transpose(ln2fm[:, :, tt * 128:(tt + 1) * 128],
                                  ln2tm[:, tt, :].rearrange(
                                      "p (c m) -> p c m", c=NK))
            for tt in range(GT)]),
            waits=(f"ln2_{g0}", f"ln2_{g0 + 1}") +
                  ((f"qkv_done_g{g - 1}",) if g >= 1 else ()),
            signal=f"ln2fm_g{g}", inc=GT * 16)

    def emit_qkv(g):
        g0 = g * GT
        qkv_tiles = {}
        for wi, wn in ((0, "q"), (1, "k"), (2, "v")):
            for ti in range(GT):
                t = g0 + ti
                for o in range(NO):
                    sig = f"{wn}ps_{t}_{o}"
                    ps, i = big_tile(
                        sig,
                        (lambda k, ti=ti: ln2fm[:, k,
                                                ti * 128:(ti + 1) * 128]),
                        (lambda k, wi=wi, o=o: w_sb[wi][
                            :, k, o * 512:(o + 1) * 512]),
                        waits=(f"ln2fm_g{g}", f"wcvt_{(wi * NK + NK) * 2 - 1}"))
                    qkv_tiles[(wn, ti, o)] = (sig, ps, i)
        S.add("tensor", lambda e: e.nop(),
              waits=(f"vps_{g0 + GT - 1}_{NO - 1}",),
              signal=f"qkv_done_g{g}")
        return qkv_tiles

    def emit_phi(g, qkv_tiles):
        g0 = g * GT
        for wn, dst in (("q", phq_tm), ("k", phk_tm)):
            for ti in range(GT):
                t = g0 + ti
                for o in range(NO):
                    sig, ps, i = qkv_tiles[(wn, ti, o)]
                    ew = [sig]
                    if o in last_e1:
                        ew.append(last_e1[o])
                    S.add("scalar", (lambda e, ps=ps, o=o: e.activation(
                        e1_sb[:, o * 512:(o + 1) * 512], ps[:, :], ACTF.Exp)),
                        waits=tuple(ew), signal=f"e1_{wn}_{t}_{o}")

                    def phi(e, ps=ps, o=o, dst=dst, ti=ti):
                        e.tensor_scalar(r_sb[:, o * 512:(o + 1) * 512],
                                        ps[:, :], 0.0, None, ALU.max)
                        return e.scalar_tensor_tensor(
                            dst[:, ti, o * 512:(o + 1) * 512],
                            e1_sb[:, o * 512:(o + 1) * 512], 1.0,
                            r_sb[:, o * 512:(o + 1) * 512], ALU.min, ALU.add)
                    pw = [f"e1_{wn}_{t}_{o}"]
                    if g >= 1:
                        pw.append(f"phifm_g{g - 1}")
                        if wn == "k":
                            pw += [f"kvps_{g0 - 1}_1", f"wps_{g0 - 1}_1"]
                    S.add("vector", phi, waits=tuple(pw),
                          signal=(f"bigfree_{i}", f"phi_{wn}_{t}_{o}"))
                    last_e1[o] = f"phi_{wn}_{t}_{o}"
        for ti in range(GT):
            t = g0 + ti
            for o in range(NO):
                sig, ps, i = qkv_tiles[("v", ti, o)]
                vw = [sig]
                if t >= 2:
                    vw.append(f"numdone_{t - 2}")
                S.add("scalar", (lambda e, ps=ps, o=o, ti=ti: e.copy(
                    vex[:, ti, 8 * o:8 * o + 8, 0:D], ps[:, :])),
                    waits=tuple(vw), signal=(f"bigfree_{i}", f"vcp_{t}_{o}"))

        S.add("vector", lambda e: e.engine_nop(),
              waits=tuple(f"phi_{wn}_{g0 + ti}_{o}" for wn in ("q", "k")
                          for ti in range(GT) for o in range(NO)),
              signal=f"phitm_g{g}")

    def emit_phifm(g):
        S.add("sync", (lambda e, g=g: [
            e.dma_start_transpose(dfm[:, :, tt * 128:(tt + 1) * 128],
                                  dtm[:, tt, :].rearrange(
                                      "p (c m) -> p c m", c=NK))
            for (dfm, dtm) in ((phq_fm, phq_tm), (phk_fm, phk_tm))
            for tt in range(GT)]),
            waits=(f"phitm_g{g}",) +
                  ((f"attn_done_g{g - 1}",) if g >= 1 else ()),
            signal=f"phifm_g{g}", inc=2 * GT * 16)

    def emit_attn(g):
        g0 = g * GT
        for ti in range(GT):
            t = g0 + ti

            w_sbf = [f"numdone_{t - 1}"] if t >= 1 else ["initS"]
            S.add("vector", (lambda e: e.tensor_copy(S_bf[:, :, :],
                                                     S_sb[:, :, :])),
                  waits=tuple(w_sbf), signal=f"sbf_{t}")

            for w in range(1):
                def kvw(e, ti=ti, w=w):
                    mm = None
                    for j in range(4):
                        hp = 4 * w + j
                        for hh in range(2):
                            h = 2 * hp + hh
                            mm = e.matmul(psK[hh * 64:hh * 64 + 64,
                                              j * 65:(j + 1) * 65],
                                          phk_tm[:, ti, h * D:(h + 1) * D],
                                          vex[:, ti, h, :], start=True,
                                          stop=True,
                                          tile_position=(0, hh * 64))
                    return mm
                wk = [f"phi_k_{t}_0", f"phi_k_{t}_1", f"vcp_{t}_0",
                      f"vcp_{t}_1"]
                if t >= 1 or w >= 1:
                    pt, pw = (t, 0) if w == 1 else (t - 1, 1)
                    wk.append(f"supd_{pt}_{pw}")
                S.add("tensor", kvw, waits=tuple(wk), signal=f"kvps_{t}_{w}")

            for w in range(2):
                def ww(e, ti=ti, w=w):
                    mm = None
                    for j in range(4):
                        hp = 4 * w + j
                        for hh in range(2):
                            h = 2 * hp + hh
                            mm = e.matmul(psW[hh * 64:hh * 64 + 64, j, :],
                                          phk_tm[:, ti, h * D:(h + 1) * D],
                                          mask_sb[:, :], start=True,
                                          stop=True,
                                          tile_position=(0, hh * 64))
                    return mm
                wk = [f"phi_k_{t}_0", f"phi_k_{t}_1"]
                if t >= 1 or w >= 1:
                    pt, pw = (t, 0) if w == 1 else (t - 1, 1)
                    wk.append(f"u_{pt}_{pw}")
                S.add("tensor", ww, waits=tuple(wk), signal=f"wps_{t}_{w}")

            for w in range(2):
                def uf(e, ti=ti, w=w):
                    for j in range(4):
                        hp = 4 * w + j
                        e.scalar_tensor_tensor(
                            u_sb[:, hp, :], psW[:, j, :],
                            S_sb[:, hp, D:D + 1],
                            phq_fm[:, hp, ti * 128:(ti + 1) * 128],
                            ALU.add, ALU.mult)
                    return e.engine_nop()
                wk = [f"wps_{t}_{w}", f"phifm_g{g}", f"sbf_{t}"]
                if t >= 1:
                    wk.append(f"densel_{t - 1}")
                S.add("vector", uf, waits=tuple(wk), signal=f"u_{t}_{w}")

            def supd0(e):
                return e.tensor_tensor(
                    S_sb[:, 0:4, :], S_sb[:, 0:4, :],
                    psK[:, 0:260].rearrange("p (j e) -> p j e", e=65),
                    ALU.add)
            S.add("vector", supd0,
                  waits=(f"kvps_{t}_0", f"u_{t}_0", f"u_{t}_1"),
                  signal=f"supd_{t}_0")

            for w in range(4):
                def atw(e, ti=ti, w=w):
                    ps = psT[w % 2]
                    hh = w % 2
                    mm = None
                    for j in range(4):
                        h = 8 * (w // 2) + 2 * j + hh
                        ce = h // 2
                        mm = e.matmul(ps[:, j, :],
                                      phk_fm[hh * 64:hh * 64 + 64, ce,
                                             ti * 128:(ti + 1) * 128],
                                      phq_fm[hh * 64:hh * 64 + 64, ce,
                                             ti * 128:(ti + 1) * 128],
                                      start=True, stop=True,
                                      tile_position=(hh * 64, 0))
                    return mm
                wk = [f"phifm_g{g}"]
                if t >= 1 or w >= 2:
                    pt, pw = (t, w - 2) if w >= 2 else (t - 1, w + 2)
                    wk.append(f"atm_{pt}_{pw}")
                S.add("tensor", atw, waits=tuple(wk), signal=f"atps_{t}_{w}")

            for w in range(4):
                def atmf(e, w=w):
                    ps = psT[w % 2]
                    return e.scalar_tensor_tensor(
                        atm[:, 4 * w:4 * w + 4, :], ps[:, :, :], 1.0,
                        mask_sb[:, :].rearrange("p (a c) -> p a c", a=1)
                        .broadcast_to([128, 4, 128]), ALU.mult, ALU.mult)
                wk = [f"atps_{t}_{w}"]
                if t >= 1:
                    wk.append(f"numdone_{t - 1}")
                S.add("vector", atmf, waits=tuple(wk), signal=f"atm_{t}_{w}")

            def supd1(e):
                return e.tensor_tensor(
                    S_sb[:, 4:8, :], S_sb[:, 4:8, :],
                    psK[:, 0:260].rearrange("p (j e) -> p j e", e=65),
                    ALU.add)
            S.add("vector", supd1, waits=(f"kvps_{t}_1",),
                  signal=f"supd_{t}_1")

            for w in range(1, 2):
                def kvw1(e, ti=ti, w=w):
                    mm = None
                    for j in range(4):
                        hp = 4 * w + j
                        for hh in range(2):
                            h = 2 * hp + hh
                            mm = e.matmul(psK[hh * 64:hh * 64 + 64,
                                              j * 65:(j + 1) * 65],
                                          phk_tm[:, ti, h * D:(h + 1) * D],
                                          vex[:, ti, h, :], start=True,
                                          stop=True,
                                          tile_position=(0, hh * 64))
                    return mm
                S.add("tensor", kvw1,
                      waits=(f"phi_k_{t}_0", f"phi_k_{t}_1", f"vcp_{t}_0",
                             f"vcp_{t}_1", f"supd_{t}_0"),
                      signal=f"kvps_{t}_1")

            def densel(e):
                mm = None
                for hp in range(8):
                    mm = e.matmul(psD[0:16, 0:128], sel_sb[:, hp, :],
                                  u_sb[:, hp, :], start=(hp == 0),
                                  stop=(hp == 7))
                return mm
            wk = [f"u_{t}_0", f"u_{t}_1"]
            if t >= 1:
                wk.append(f"rdenx_{t - 1}")
            S.add("tensor", densel, waits=tuple(wk), signal=f"densel_{t}")

            S.add("scalar", (lambda e: e.copy(den_f[:, :],
                                              psD[0:16, 0:128])),
                  waits=(f"densel_{t}",) + ((f"denT_{t - 1}",) if t >= 1
                                            else (f"wcvt_{4 * NK * 2 - 1}",)),
                  signal=f"denf_{t}")

            S.add("tensor", (lambda e: e.transpose(
                psD[:, 128:144], den_f[:, :], idn_sb[0:16, 0:16])),
                waits=(f"denf_{t}",), signal=f"denT_{t}")

            def rdenx(e, ti=ti):
                ssync(e, e.tensor_scalar(rden[:, ti, :], psD[:, 128:144],
                                         1e-6, None, ALU.add))
                return e.reciprocal(rden[:, ti, :], rden[:, ti, :])
            S.add("vector", rdenx, waits=(f"denT_{t}",), signal=f"rdenx_{t}")

            for w in range(2):
                def numw(e, ti=ti, w=w):
                    mm = None
                    for j in range(8):
                        h = 8 * w + j
                        ce, hh = h // 2, h % 2
                        slot = 4 * ((h % 2) + 2 * (h // 8)) + (h % 8) // 2
                        e.matmul(psN[:, j, :], atm[:, slot, :],
                                 vex[:, ti, h, 0:D], start=True, stop=False)
                        mm = e.matmul(psN[:, j, :],
                                      phq_fm[hh * 64:hh * 64 + 64, ce,
                                             ti * 128:(ti + 1) * 128],
                                      S_bf[hh * 64:hh * 64 + 64, h // 2,
                                           0:D],
                                      start=False, stop=True,
                                      tile_position=(hh * 64, 0))
                    return mm
                wk = [f"atm_{t}_{2 * w}", f"atm_{t}_{2 * w + 1}",
                      f"sbf_{t}", f"vcp_{t}_0", f"vcp_{t}_1",
                      f"phifm_g{g}"]
                if t >= 1 or w >= 1:
                    pt, pw = (t, 0) if w == 1 else (t - 1, 1)
                    wk.append(f"ynum_{pt}_{pw}")
                S.add("tensor", numw, waits=tuple(wk), signal=f"nps_{t}_{w}")
            S.add("tensor", lambda e: e.nop(),
                  waits=(f"nps_{t}_0", f"nps_{t}_1"), signal=f"numdone_{t}")

            for w in range(2):
                def ynum(e, ti=ti, w=w):
                    return e.tensor_tensor(
                        y_tm[:, ti, w * 512:(w + 1) * 512].rearrange(
                            "p (h d) -> p h d", d=D),
                        psN[:, :, :],
                        rden[:, ti, 8 * w:8 * w + 8].rearrange(
                            "p (h o) -> p h o", o=1).broadcast_to(
                                [128, 8, D]), ALU.mult)
                yw = [f"nps_{t}_{w}", f"rdenx_{t}"]
                if g >= 1:
                    yw.append(f"yfm_g{g - 1}")
                S.add("vector", ynum, waits=tuple(yw),
                      signal=f"ynum_{t}_{w}")

        S.add("vector", lambda e: e.engine_nop(),
              waits=(f"numdone_{g0 + GT - 1}",), signal=f"attn_done_g{g}")
        S.add("vector", lambda e: e.engine_nop(),
              waits=(f"ynum_{g0}_1", f"ynum_{g0 + 1}_1"),
              signal=f"ytm_g{g}")

    def emit_yfm(g):
        S.add("sync", (lambda e, g=g: [
            e.dma_start_transpose(y_fm[:, :, tt * 128:(tt + 1) * 128],
                                  y_tm[:, tt, :].rearrange(
                                      "p (c m) -> p c m", c=NK))
            for tt in range(GT)]),
            waits=(f"ytm_g{g}",) +
                  ((f"p_done_g{g - 1}",) if g >= 1 else ()),
            signal=f"yfm_g{g}", inc=GT * 16)

    def emit_P(g):
        g0 = g * GT
        p_tiles = []
        for ti in range(GT):
            t = g0 + ti
            for o in range(NO):
                sig = f"pps_{t}_{o}"
                ps, i = big_tile(
                    sig,
                    (lambda k, ti=ti: y_fm[:, k, ti * 128:(ti + 1) * 128]),
                    (lambda k, o=o: w_sb[3][:, k, o * 512:(o + 1) * 512]),
                    waits=(f"yfm_g{g}", f"wcvt_{(3 * NK + NK) * 2 - 1}",
                           "init_v"),
                    outer=(lambda o=o: (ones_sb[0:1, :],
                                        bp_sb[0:1, o * 512:(o + 1) * 512])))
                p_tiles.append((t, sig, ps, i, o))
        S.add("tensor", lambda e: e.nop(),
              waits=(f"pps_{g0 + GT - 1}_{NO - 1}",),
              signal=f"p_done_g{g}")

        for t, sig, ps, i, o in p_tiles:
            def pev(e, ps=ps, t=t, o=o):
                ssync(e, e.scalar_tensor_tensor(
                    x_sb[:, t % RT, o * 512:(o + 1) * 512], ps[:, :], 0.0,
                    x_sb[:, t % RT, o * 512:(o + 1) * 512], ALU.add, ALU.add,
                    accum_out=tA[:, t:t + 1]))
                if o == 0:
                    return e.tensor_copy(sxC[:, t:t + 1], tA[:, t:t + 1])
                return e.tensor_tensor(sxC[:, t:t + 1], sxC[:, t:t + 1],
                                       tA[:, t:t + 1], ALU.add)
            S.add("vector", pev, waits=(sig,),
                  signal=(f"bigfree_{i}", f"pev_{t}_{o}"))

    def emit_store(g):
        s0 = (g * GT) % RT
        S.add("sync", (lambda e, g=g, s0=s0: [
            e.dma_start(out=out_h.rearrange("p (t e) -> p t e", e=E)[
                :, g * GT:(g + 1) * GT, :],
                in_=x_sb[:, s0:s0 + GT, :])]),
            waits=tuple(f"ffnev_f2_{g * GT + ti}_{o}"
                        for ti in range(GT) for o in range(NO)),
            signal=f"store_{g}", inc=16)

    def emit_w(lo, hi):
        for li, wts, fn in wl_list[lo:hi]:
            S.add("sync", fn, waits=tuple(wts), signal=f"wload_{li}", inc=16)
        for li, fn in wc_list[lo:hi]:
            S.add("scalar", fn, waits=(f"wload_{li}",), signal=f"wcvt_{li}")

    emit_w(0, 48)
    t_f1 = {}
    emit_sums(0)
    emit_sums(1)
    emit_quant("f1", 0, sxA)
    t_f1[0] = emit_fT_mm("f1", 0)
    emit_quant("f1", 1, sxA)
    for t, tl in t_f1[0]:
        ffn_evac(t, "f1", tl, sxB)
    t_f1[1] = emit_fT_mm("f1", 1)
    emit_ln2(0)
    for t, tl in t_f1[1]:
        ffn_evac(t, "f1", tl, sxB)
    emit_ln2fm(0)
    qk0 = emit_qkv(0)
    emit_phi(0, qk0)
    emit_w(48, 64)
    S.add("scalar", (lambda e: e.activation(S_sb[:, :, :], S_sb[:, :, :],
                                            ACTF.Copy, scale=0.0, bias=0.0)),
          waits=(f"wcvt_{4 * NK * 2 - 1}",), signal="initS")
    emit_phifm(0)

    t_f2 = {}
    for g in range(NG):
        if g + 2 < NG:
            emit_sums(g + 2)
            emit_quant("f1", g + 2, sxA)
            emit_fT_cast("f1", g + 2)
        if g >= 1 and g + 1 < NG:
            for t, tl in t_f1[g + 1]:
                ffn_evac(t, "f1", tl, sxB)
        if g >= 1:
            for t, tl in t_f2[g - 1]:
                ffn_evac(t, "f2", tl, None)
            emit_store(g - 1)
            gl = g - 1 + RT // GT
            if gl < NG:
                emit_xload(gl, waits=(f"store_{g - 1}",))
        if g + 1 < NG:
            emit_ln2(g + 1)
            emit_ln2fm(g + 1)
        emit_attn(g)
        if g + 1 < NG:
            qk = emit_qkv(g + 1)
            emit_phi(g + 1, qk)
        emit_yfm(g)
        emit_P(g)
        if g + 2 < NG:
            t_f1[g + 2] = emit_fmm("f1", g + 2)
        emit_quant("f2", g, sxC)
        t_f2[g] = emit_fT_mm("f2", g)
        if g + 1 < NG:
            emit_phifm(g + 1)

    for t, tl in t_f2[NG - 1]:
        ffn_evac(t, "f2", tl, None)
    emit_store(NG - 1)

    S.add("sync", lambda e: e.nop(),
          waits=tuple(f"store_{g}" for g in range(NG)), signal=None)

    S.resolve()
    sems = {name: nc.alloc_semaphore(f"sem_{name}")
            for name in ENGS + ("gpsimd_dma",)}
    with nc.Block() as block:
        S.emit(block, sems)
    return nc


_NC_CACHE = {}


def _get_nc():
    if "nc" not in _NC_CACHE:
        _NC_CACHE["nc"] = build_program()
    return _NC_CACHE["nc"]


def _consts():
    s, c = np.meshgrid(np.arange(128), np.arange(128), indexing="ij")
    mask = (s <= c).astype(ml_dtypes.bfloat16)
    idn = np.eye(128, dtype=np.float32)
    sel = np.zeros((128, 8, 16), dtype=ml_dtypes.bfloat16)
    for p in range(128):
        for hp in range(8):
            sel[p, hp, 2 * hp + (p // 64)] = 1.0
    ones = np.ones((1, 128), dtype=ml_dtypes.bfloat16)
    cp = np.zeros((128, 4), dtype=np.float32)
    cp[:, 0] = -1.0
    cp[:, 1] = -0.5
    cp[:, 2] = 80.5
    cp[:, 3] = -80.0
    return mask, idn, sel, ones, cp


def _layout_w(wT):
    return np.ascontiguousarray(
        wT.reshape(NK, 128, 2, 512).transpose(0, 2, 1, 3).reshape(2 * E,
                                                                  E // 2))


def _layout_x(x):
    return np.ascontiguousarray(
        x.reshape(NT, 128, E).transpose(1, 0, 2).reshape(128, NT * E))


def _unlayout_out(o):
    return np.ascontiguousarray(
        o.reshape(128, NT, E).transpose(1, 0, 2).reshape(T, E))


def kernel(L0_attn_x, L0_attn_Wq, L0_attn_Wk, L0_attn_Wv, L0_attn_Wp,
           L0_attn_bp, L0_ffn_weight, L0_ffn_weight_scale, L0_ffn_bias):
    nc = _get_nc()
    mask, idn, sel, ones, cp = _consts()
    x = np.ascontiguousarray(np.asarray(L0_attn_x, dtype=np.float32))
    wqT = _layout_w(np.asarray(L0_attn_Wq, np.float32).T)
    wkT = _layout_w(np.asarray(L0_attn_Wk, np.float32).T)
    wvT = _layout_w(np.asarray(L0_attn_Wv, np.float32).T)
    wpT = _layout_w(np.asarray(L0_attn_Wp, np.float32).T)
    w8 = np.asarray(L0_ffn_weight)
    w8T = np.ascontiguousarray(w8.view(np.uint8).T)
    bp = np.ascontiguousarray(np.asarray(L0_attn_bp, np.float32).reshape(1, E))
    fb = np.ascontiguousarray(np.asarray(L0_ffn_bias, np.float32).reshape(1, E))
    ws = np.full((128, 1), np.float32(np.asarray(L0_ffn_weight_scale)),
                 dtype=np.float32)

    shared = {"wqT": wqT, "wkT": wkT, "wvT": wvT, "wpT": wpT, "w8T": w8T,
              "bp": bp, "fb": fb, "ws": ws, "cp": cp, "maskT": mask,
              "idn": idn, "sel": sel, "ones": ones}
    in_maps = [dict(shared, x=_layout_x(x[b])) for b in range(8)]
    res = run_bass_kernel_spmd(nc, in_maps, core_ids=list(range(8)))
    out = np.stack([_unlayout_out(np.asarray(r["out"])) for r in res.results],
                   axis=0)
    return out.astype(np.float32)


# version 13
# speedup vs baseline: 1.0678x; 1.0218x over previous
import numpy as np
import ml_dtypes
import concourse.bass as bass
import concourse.mybir as mybir
from concourse.bass_utils import run_bass_kernel_spmd

F32 = mybir.dt.float32
I32 = mybir.dt.int32
BF16 = mybir.dt.bfloat16
U8 = mybir.dt.uint8
FP8 = mybir.dt.float8e4
ALU = mybir.AluOpType
ACTF = mybir.ActivationFunctionType
AXX = mybir.AxisListType.X

T = 2048
E = 1024
H = 16
D = 64
NT = T // 128
GT = 2
NG = NT // GT
NK = E // 128
NO = 2
RT = 10

ENGS = ("sync", "tensor", "vector", "scalar", "gpsimd")
PHASE = 4


class Sched:
    def __init__(self):
        self.items = []
        self.M = {}

    def add(self, eng, fn, waits=(), signal=None, inc=1, sem=None):
        self.items.append((eng, tuple(waits), signal, inc, fn, sem or eng))

    def resolve(self):
        val = {}
        for eng, waits, signal, inc, fn, sk in self.items:
            if signal is not None:
                val[sk] = val.get(sk, 0) + inc
                names = signal if isinstance(signal, tuple) else (signal,)
                for nm in names:
                    assert nm not in self.M, f"dup signal {nm}"
                    self.M[nm] = (sk, val[sk])
        for eng, waits, signal, inc, fn, sk in self.items:
            for w in waits:
                assert w in self.M, f"unresolved wait {w} (engine {eng})"

    def emit(self, block, sems):
        per_eng = {e: [] for e in ENGS}
        for it in self.items:
            per_eng[it[0]].append(it)

        def make_runner(eng_name):
            def run(e):
                for _, waits, signal, inc, fn, sk in per_eng[eng_name]:
                    for w in waits:
                        psk, v = self.M[w]
                        e.wait_ge(sems[psk], v)
                    insts = fn(e)
                    if signal is not None:
                        if isinstance(insts, list):
                            for inst in insts:
                                inst.then_inc(sems[sk], 16)
                        else:
                            insts.then_inc(sems[sk], 1)
            return run

        for name in ENGS:
            if per_eng[name]:
                getattr(block, name)(make_runner(name))


def build_program():
    nc = bass.Bass("TRN2", target_bir_lowering=False, debug=False,
                   detect_race_conditions=False)

    x_h = nc.dram_tensor("x", [128, NT * E], F32, kind="ExternalInput").ap()
    wT_h = [nc.dram_tensor(n, [2 * E, E // 2], F32, kind="ExternalInput").ap()
            for n in ("wqT", "wkT", "wvT", "wpT")]
    w8T_h = nc.dram_tensor("w8T", [E, E], U8, kind="ExternalInput").ap()
    bp_h = nc.dram_tensor("bp", [1, E], F32, kind="ExternalInput").ap()
    fb_h = nc.dram_tensor("fb", [1, E], F32, kind="ExternalInput").ap()
    ws_h = nc.dram_tensor("ws", [128, 1], F32, kind="ExternalInput").ap()
    cp_h = nc.dram_tensor("cp", [128, 4], F32, kind="ExternalInput").ap()
    mask_h = nc.dram_tensor("maskT", [128, 128], BF16, kind="ExternalInput").ap()
    idn_h = nc.dram_tensor("idn", [128, 128], F32, kind="ExternalInput").ap()
    sel_h = nc.dram_tensor("sel", [128, 8, 16], BF16, kind="ExternalInput").ap()
    ones_h = nc.dram_tensor("ones", [1, 128], BF16, kind="ExternalInput").ap()
    out_h = nc.dram_tensor("out", [128, NT * E], F32,
                           kind="ExternalOutput").ap()

    A = nc.alloc_sbuf_tensor
    x_sb = A("x_sb", [128, RT, E], F32)
    w_sb = [A(f"w{n}_sb", [128, NK, E], BF16) for n in "qkvp"]
    w8_sb = A("w8_sb", [128, NK, E], FP8)
    w8u = w8_sb.ap().bitcast(U8)
    qf8a = A("qf8a", [128, NK, GT * 128], FP8)
    qf8b = A("qf8b", [128, NK, GT * 128], FP8)
    qf8w = {"f1": qf8a, "f2": qf8b}
    ralpha = A("ralpha", [128, NT], BF16)
    iarb = A("iarb", [1, 2, GT * 128], BF16)
    stgS = A("stgS", [128, 544], F32)
    stg2 = A("stg2", [128, 512], F32)
    stg = [stgS[:, 0:512], stg2[:, :]]

    ws_sb = A("ws_sb", [128, 1], F32)
    ws2_sb = A("ws2_sb", [128, 1], F32)
    cp_sb = A("cp_sb", [128, 4], F32)
    mask_sb = A("mask_sb", [128, 128], BF16)
    idn_sb = A("idn_sb", [128, 128], F32)
    sel_sb = A("sel_sb", [128, 8, 16], BF16)
    ones_sb = A("ones_sb", [1, 128], BF16)
    bp_sb = A("bp_sb", [1, E], BF16)
    fbh_sb = A("fbh_sb", [1, E], BF16)

    sxA = A("sxA", [128, NT], F32)
    sxB = A("sxB", [128, NT], F32)
    sxC = A("sxC", [128, NT], F32)
    mu_c = A("mu_c", [128, NT], F32)
    ssq_c = A("ssq_c", [128, NT], F32)
    mxa = A("mxa", [128, NT], F32)
    mxb = A("mxb", [128, NT], F32)
    rstd_c = A("rstd_c", [128, NT], F32)
    amax_c = A("amax_c", [128, NT], F32)
    rq_c = A("rq_c", [128, NT], F32)
    alpha_c = A("alpha_c", [128, NT], F32)
    tA = A("tA", [128, NT], F32)
    tB = A("tB", [128, NT], F32)

    q_dev = A("q_dev", [128, E], F32)
    q_gh = A("q_gh", [128, E], I32)
    q_t7 = A("q_t7", [128, E], F32)
    q_ub = A("q_ub", [128, E], I32)
    q_a1 = A("q_a1", [128, E], F32)
    q_a2 = A("q_a2", [128, E], BF16)
    sq_scr = q_a2

    qtm = A("qtm", [128, GT, E], BF16)
    fm_sh = A("fm_sh", [128, NK, GT * 128], BF16)
    qfm = fm_sh
    ln2fm = A("ln2fm_sb", [128, NK, GT * 128], BF16)
    y_fm = A("y_fm_sb", [128, NK, GT * 128], BF16)
    phq_fm = A("phq_fm_sb", [128, NK, GT * 128], BF16)

    ln2tm = A("ln2tm", [128, GT, E], BF16)
    y_tm = A("y_tm_sb", [128, GT, E], BF16)
    e1_sb = ln2tm[:, 0, :]
    r_sb = ln2tm[:, 1, :]
    ph_tm2 = A("ph_tm2", [128, 2, GT, E], BF16)
    phq_tm = ph_tm2[:, 0]
    phk_tm = ph_tm2[:, 1]
    phk_fm = A("phk_fm", [128, NK, GT * 128], BF16)
    vex = A("vex", [128, 2, H, D + 1], BF16)
    atm = A("atm_sb", [128, 16, 128], BF16)
    u_sb = A("u_sb", [128, 8, 128], BF16)
    S_sb = stgS[:, 0:520].rearrange("p (a b) -> p a b", b=D + 1)
    S_bf = A("S_bf", [128, 8, D + 1], BF16)
    den_f = stg2[0:16, 0:128]
    rden = A("rden", [128, GT, 16], F32)

    d_i = q_ub
    d_e = q_gh
    d_f = q_t7[:, :].bitcast(I32)
    d_s = q_dev
    d_t1 = q_a1
    d_t2 = q_a2
    d_p2 = q_ub

    psA = nc.alloc_psum_tensor("psA", [128, 512], F32)
    psB = nc.alloc_psum_tensor("psB", [128, 512], F32)
    psK = nc.alloc_psum_tensor("psK", [128, 512], F32)
    psT1 = nc.alloc_psum_tensor("psT1", [128, 4, 128], F32)
    psT2 = nc.alloc_psum_tensor("psT2", [128, 4, 128], F32)
    psW = nc.alloc_psum_tensor("psW", [128, 4, 128], F32)
    psN = nc.alloc_psum_tensor("psN", [128, 8, D], F32)
    psD = nc.alloc_psum_tensor("psD", [128, 512], F32)
    psAB = [psA, psB]
    psT = [psT1, psT2]

    S = Sched()

    ss_sem = {}
    ss_val = {}

    def ssync(e, inst):
        nm = str(e.engine)
        if nm not in ss_sem:
            ss_sem[nm] = nc.alloc_semaphore(f"ss_{len(ss_sem)}")
            ss_val[nm] = 0
        ss_val[nm] += 1
        inst.then_inc(ss_sem[nm], 1)
        e.wait_ge(ss_sem[nm], ss_val[nm])
        return inst

    S.add("sync", lambda e: [
        e.dma_start(out=ws_sb[:, :], in_=ws_h),
        e.dma_start(out=cp_sb[:, :], in_=cp_h),
        e.dma_start(out=mask_sb[:, :], in_=mask_h),
        e.dma_start(out=idn_sb[:, :], in_=idn_h),
        e.dma_start(out=sel_sb[:, :, :], in_=sel_h),
        e.dma_start(out=ones_sb[:, :], in_=ones_h),
        e.dma_start(out=q_dev[0:1, 0:E], in_=bp_h),
        e.dma_start(out=q_t7[0:1, 0:E], in_=fb_h),
    ], signal="smallc", inc=8 * 16)

    def emit_xload(g, waits=()):
        s0 = (g * GT) % RT
        S.add("sync", (lambda e, g=g, s0=s0: [
            e.dma_start(out=x_sb[:, s0:s0 + GT, :],
                        in_=x_h.rearrange("p (t e) -> p t e", e=E)[
                            :, g * GT:(g + 1) * GT, :])]),
            waits=waits, signal=f"x_in_g{g}", inc=16)

    sums_done = set()

    def emit_sums(g):
        if g in sums_done:
            return
        sums_done.add(g)

        def sums_g(e, g=g):
            for ti in range(GT):
                t = g * GT + ti
                e.reduce_sum(sxA[:, t:t + 1], x_sb[:, t % RT, :], AXX)
            return e.engine_nop()
        S.add("vector", sums_g, waits=(f"x_in_g{g}",), signal=f"sx1_g{g}")

    for g in range(min(NG, 2)):
        emit_xload(g)
    S.add("sync", lambda e: [
        e.dma_start(out=w8u[:, :, :],
                    in_=w8T_h.rearrange("(c p) o -> p c o", p=128)),
    ], waits=("smallc",), signal="consts", inc=16)
    for g in range(2, min(NG, RT // GT)):
        emit_xload(g)

    def init_v(e):
        e.tensor_scalar(ws2_sb[:, :], ws_sb[:, :], 2.0 / 448.0, None,
                        ALU.mult)
        e.tensor_copy(bp_sb[0:1, :], q_dev[0:1, 0:E])
        e.tensor_scalar(fbh_sb[0:1, :], q_t7[0:1, 0:E], 0.5, None, ALU.mult)
        return e.memset(vex[:, :, :, D:D + 1], 1.0)
    S.add("vector", init_v, waits=("smallc",), signal="init_v")

    def make_dec(c):
        def dec(e):
            e.tensor_copy(d_i[:, :], w8u[:, c, :])
            e.tensor_scalar(d_e[:, :], d_i[:, :], 0x78, 3, ALU.bitwise_and,
                            ALU.logical_shift_right)
            e.tensor_scalar(d_f[:, :], d_i[:, :], 0x7, None, ALU.bitwise_and)
            e.tensor_scalar(d_p2[:, :], d_i[:, :], 7, None,
                            ALU.logical_shift_right)
            e.tensor_scalar(d_s[:, :], d_p2[:, :], -1.0, 0.5, ALU.mult,
                            ALU.add)
            e.tensor_scalar(d_t1[:, :], d_e[:, :], 1.0, 8.0, ALU.min,
                            ALU.mult)
            e.tensor_tensor(d_t1[:, :], d_t1[:, :], d_f[:, :], ALU.add)
            e.tensor_scalar(d_t2[:, :], d_e[:, :], 1.0, 117.0, ALU.max,
                            ALU.add)
            e.tensor_scalar(d_p2[:, :], d_t2[:, :], 8388608.0, None, ALU.mult)
            e.tensor_tensor(d_t1[:, :], d_t1[:, :], d_p2[:, :].bitcast(F32),
                            ALU.mult)
            return e.tensor_tensor(w8_sb[:, c, :], d_t1[:, :], d_s[:, :],
                                   ALU.mult)
        return dec

    wload_items = []
    wcvt_items = []
    for wi in range(4):
        for c in range(NK):
            for hf in range(2):
                li = (wi * NK + c) * 2 + hf
                wts = ["init_v"] if li < 2 else [f"wcvt_{li - 2}"]
                wload_items.append((li, wts, (lambda e, wi=wi, c=c, hf=hf,
                                              li=li: [
                    e.dma_start(out=stg[li % 2][:, :],
                                in_=wT_h[wi][(c * 2 + hf) * 128:
                                             (c * 2 + hf + 1) * 128, :])])))
                wcvt_items.append((li, (lambda e, wi=wi, c=c, hf=hf, li=li:
                                        e.copy(
                    w_sb[wi][:, c, hf * 512:(hf + 1) * 512],
                    stg[li % 2][:, :]))))

    big_i = [0]

    def big_tile(sig, lhs_fn, rhs_fn, waits=(), outer=None):
        i = big_i[0]
        big_i[0] += 1
        ps = psAB[i % 2]
        w = list(waits)
        if i >= 2:
            w.append(f"bigfree_{i - 2}")

        def fn(e, ps=ps):
            mm = None
            for k in range(NK):
                mm = e.matmul(ps[:, :], lhs_fn(k), rhs_fn(k), start=(k == 0),
                              stop=(k == NK - 1 and outer is None))
            if outer is not None:
                lhs, rhs = outer()
                mm = e.matmul(ps[:, :], lhs, rhs, start=False, stop=True)
            return mm
        S.add("tensor", fn, waits=w, signal=sig)
        return ps, i

    prevq = {"sig": None}

    def ffn_quant_tile(t, which, sx_src, extra_wait=()):
        ti = t % GT
        tg = f"{which}_{t}"

        def stats1(e):
            ssync(e, e.tensor_scalar(mu_c[:, t:t + 1], sx_src[:, t:t + 1],
                                     1.0 / E, None, ALU.mult))
            e.tensor_scalar(q_dev[:, :], x_sb[:, t % RT, :],
                            mu_c[:, t:t + 1],
                            None, ALU.subtract, ALU.min,
                            accum_out=mxa[:, t:t + 1])
            return e.tensor_scalar(q_dev[:, :], x_sb[:, t % RT, :],
                                   mu_c[:, t:t + 1], None, ALU.subtract,
                                   ALU.max, accum_out=mxb[:, t:t + 1])
        S.add("vector", stats1, waits=tuple(extra_wait), signal=f"dev_{tg}")

        S.add("scalar", (lambda e: e.activation(
            sq_scr[:, :], q_dev[:, :], ACTF.Square,
            accum_out=ssq_c[:, t:t + 1])), waits=(f"dev_{tg}",),
            signal=f"ssq_{tg}")

        def stats2(e):
            ssync(e, e.scalar_tensor_tensor(amax_c[:, t:t + 1],
                                            mxa[:, t:t + 1], -1.0,
                                            mxb[:, t:t + 1], ALU.mult,
                                            ALU.max))
            ssync(e, e.tensor_scalar(tA[:, t:t + 1], amax_c[:, t:t + 1],
                                     1e-30, None, ALU.max))
            ssync(e, e.reciprocal(rq_c[:, t:t + 1], tA[:, t:t + 1]))
            e.tensor_scalar(tB[:, t:t + 1], ssq_c[:, t:t + 1],
                            1.0 / E, 1e-5, ALU.mult, ALU.add)
            return e.tensor_scalar(q_dev[:, :], q_dev[:, :],
                                   rq_c[:, t:t + 1], 448.0, ALU.mult,
                                   ALU.mult)
        S.add("vector", stats2, waits=(f"ssq_{tg}",),
              signal=(f"st2_{tg}", f"y_{tg}"))

        def rstd_f(e):
            ssync(e, e.activation(tB[:, t:t + 1], tB[:, t:t + 1], ACTF.Ln))
            return e.activation(rstd_c[:, t:t + 1], tB[:, t:t + 1], ACTF.Exp,
                                bias=0.0, scale=cp_sb[:, 1:2])
        S.add("scalar", rstd_f, waits=(f"st2_{tg}",), signal=f"rstd_{tg}")

        def expo(e):
            e.tensor_scalar(q_gh[:, :], q_dev[:, :].bitcast(I32), 0x7F800000,
                            None, ALU.bitwise_and)
            e.tensor_scalar(q_gh[:, :], q_gh[:, :], 33554432.0, 981467136.0,
                            ALU.subtract, ALU.max)
            e.tensor_scalar(q_a1[:, :].bitcast(I32), q_gh[:, :], -1.0,
                            2122317824.0, ALU.mult, ALU.add)
            e.tensor_tensor(q_t7[:, :], q_dev[:, :], q_a1[:, :], ALU.mult)
            e.tensor_scalar(q_t7[:, :], q_t7[:, :], 80.5, None, ALU.add)
            e.tensor_scalar(q_ub[:, :], q_t7[:, :].bitcast(I32), -131072,
                            None, ALU.bitwise_and)
            return e.tensor_tensor(q_dev[:, :], q_t7[:, :],
                                   q_ub[:, :].bitcast(F32), ALU.subtract)
        S.add("vector", expo, waits=(f"y_{tg}",), signal=f"dd_{tg}")

        def signs(e):
            e.activation(q_a1[:, :], q_dev[:, :], ACTF.Sign)
            return e.activation(q_a2[:, :], q_dev[:, :], ACTF.Sign,
                                bias=cp_sb[:, 0:1], scale=1.0)
        S.add("scalar", signs, waits=(f"dd_{tg}",), signal=f"sgn_{tg}")

        def tail2(e):
            ssync(e, e.tensor_tensor(tA[:, t:t + 1], rstd_c[:, t:t + 1],
                                     amax_c[:, t:t + 1], ALU.mult))
            ssync(e, e.scalar_tensor_tensor(alpha_c[:, t:t + 1],
                                            tA[:, t:t + 1], 1e-12,
                                            ws2_sb[:, 0:1], ALU.max,
                                            ALU.mult))
            with nc.allow_low_precision(reason="1/alpha is a bf16 matmul "
                                         "operand (bias outer product)"):
                e.reciprocal(ralpha[:, t:t + 1], alpha_c[:, t:t + 1])
            e.tensor_scalar(q_t7[:, :], q_ub[:, :].bitcast(F32), -80.0,
                            None, ALU.add)
            e.tensor_tensor(q_a1[:, :], q_a1[:, :], q_a2[:, :], ALU.add)
            e.scalar_tensor_tensor(q_t7[:, :], q_a1[:, :], 0.5,
                                   q_t7[:, :], ALU.mult, ALU.add)
            return e.tensor_tensor(qtm[:, ti, :], q_t7[:, :],
                                   q_gh[:, :].bitcast(F32), ALU.mult)
        S.add("vector", tail2, waits=(f"sgn_{tg}", f"rstd_{tg}"),
              signal=(f"tl_{tg}", f"qtm_{tg}"))

    def ffn_mm(t, which, g):
        ti = t % GT
        par = ffn_seq_par[which, g]
        tiles = []
        for o in range(NO):
            sig = f"ffnps_{which}_{t}_{o}"
            ps, i = big_tile(
                sig,
                (lambda k, ti=ti, q8=qf8w[which]:
                 q8[:, k, ti * 128:(ti + 1) * 128]),
                (lambda k, o=o: w8_sb[:, k, o * 512:(o + 1) * 512]),
                waits=(f"qf8c_{which}_g{g}", f"iarow_{which}_g{g}",
                       f"wdec_{NK - 1}", "init_v"),
                outer=(lambda o=o, ti=ti, par=par: (
                    iarb[0:1, par, ti * 128:(ti + 1) * 128],
                    fbh_sb[0:1, o * 512:(o + 1) * 512])))
            tiles.append((sig, ps, i, o))
        return tiles

    ffn_seq = [0]
    ffn_seq_par = {}

    def ffn_qcast(which, g, qf8_free_prev):
        si = ffn_seq[0]
        ffn_seq[0] += 1
        par = si % 2
        ffn_seq_par[which, g] = par
        g0 = g * GT
        S.add("scalar", (lambda e, q8=qf8w[which]: e.copy(
            q8[:, :, :], qfm[:, :, :])),
              waits=(f"qfm_{which}_g{g}",) + qf8_free_prev,
              signal=f"qf8c_{which}_g{g}")

        def iarow(e, g0=g0, par=par):
            return [e.dma_start(out=iarb[0:1, par, ti * 128:(ti + 1) * 128],
                                in_=ralpha[:, g0 + ti:g0 + ti + 1])
                    for ti in range(GT)]
        S.add("gpsimd", iarow,
              waits=tuple(f"qtm_{which}_{g0 + ti}" for ti in range(GT)) +
                    qf8_free_prev,
              signal=f"iarow_{which}_g{g}", inc=GT * 16, sem="gpsimd_dma")

    def ffn_evac(t, which, tiles, sx_next):
        tg = f"{which}_{t}"
        for sig, ps, i, o in tiles:
            def ev(e, ps=ps, o=o, t=t):
                ssync(e, e.scalar_tensor_tensor(
                    x_sb[:, t % RT, o * 512:(o + 1) * 512], ps[:, :],
                    alpha_c[:, t:t + 1],
                    x_sb[:, t % RT, o * 512:(o + 1) * 512], ALU.mult,
                    ALU.add, accum_out=tA[:, t:t + 1]))
                if sx_next is not None:
                    if o == 0:
                        return e.tensor_copy(sx_next[:, t:t + 1],
                                             tA[:, t:t + 1])
                    return e.tensor_tensor(sx_next[:, t:t + 1],
                                           sx_next[:, t:t + 1],
                                           tA[:, t:t + 1], ALU.add)
                return e.tensor_copy(tB[:, t:t + 1], tA[:, t:t + 1])
            S.add("vector", ev, waits=(sig,),
                  signal=(f"bigfree_{i}", f"ffnev_{tg}_{o}"))

    wc_list = list(wcvt_items)
    wl_list = list(wload_items)
    dec_iter = iter(range(NK))
    seqst = {"qtm": None, "qfm": None, "mm": None,
             "mm_f1": None, "mm_f2": None}

    def emit_quant(which, g, sx_src):
        g0 = g * GT
        for ti in range(GT):
            t = g0 + ti
            if which == "f1":
                ew = [f"sx1_g{g}"]
            else:
                ew = [f"pev_{t}_{NO - 1}", f"ssq2_{t}"]
            if seqst["qtm"] is not None:
                ew.append(seqst["qtm"])
            ffn_quant_tile(t, which, sx_src, extra_wait=tuple(ew))
            if which == "f1":
                for _ in range(2):
                    try:
                        c = next(dec_iter)
                    except StopIteration:
                        break
                    S.add("vector", make_dec(c),
                          waits=("consts",) if c == 0 else (),
                          signal=f"wdec_{c}")
        seqst["qtm"] = f"qfm_{which}_g{g}"

    def emit_fT_cast(which, g):
        tw = [f"qtm_{which}_{g * GT}", f"qtm_{which}_{g * GT + 1}"]
        if seqst["qfm"] is not None:
            tw.append(seqst["qfm"])
        S.add("sync", (lambda e, g=g: [
            e.dma_start_transpose(qfm[:, :, tt * 128:(tt + 1) * 128],
                                  qtm[:, tt, :].rearrange(
                                      "p (c m) -> p c m", c=NK))
            for tt in range(GT)]),
            waits=tuple(tw), signal=f"qfm_{which}_g{g}", inc=GT * 16)
        seqst["qfm"] = f"qf8c_{which}_g{g}"
        mmprev = seqst["mm_" + which]
        ffn_qcast(which, g,
                  qf8_free_prev=(mmprev,) if mmprev else ("consts",))

    def emit_fmm(which, g):
        tiles = [(g * GT + ti, ffn_mm(g * GT + ti, which, g))
                 for ti in range(GT)]
        S.add("tensor", lambda e: e.nop(),
              waits=(f"ffnps_{which}_{g * GT + GT - 1}_{NO - 1}",),
              signal=f"ffn_mm_done_{which}_g{g}")
        seqst["mm_" + which] = f"ffn_mm_done_{which}_g{g}"
        return tiles

    def emit_fT_mm(which, g):
        emit_fT_cast(which, g)
        return emit_fmm(which, g)

    last_e1 = {}

    def emit_ln2(g):
        g0 = g * GT
        for ti in range(GT):
            t = g0 + ti

            def dev2(e, t=t):
                ssync(e, e.tensor_scalar(mu_c[:, t:t + 1], sxB[:, t:t + 1],
                                         1.0 / E, None, ALU.mult))
                return e.tensor_scalar(q_dev[:, :], x_sb[:, t % RT, :],
                                       mu_c[:, t:t + 1], None, ALU.subtract)
            S.add("vector", dev2, waits=(f"ffnev_f1_{t}_{NO - 1}",),
                  signal=f"dev2_{t}")
            S.add("scalar", (lambda e, t=t: e.activation(
                sq_scr[:, :], q_dev[:, :], ACTF.Square,
                accum_out=ssq_c[:, t:t + 1])), waits=(f"dev2_{t}",),
                signal=f"ssq2_{t}")

            def var2(e, t=t):
                return e.tensor_scalar(tB[:, t:t + 1], ssq_c[:, t:t + 1],
                                       1.0 / E, 1e-5, ALU.mult, ALU.add)
            S.add("vector", var2, waits=(f"ssq2_{t}",), signal=f"var2_{t}")

            def rstd2(e, t=t):
                ssync(e, e.activation(tB[:, t:t + 1], tB[:, t:t + 1],
                                      ACTF.Ln))
                return e.activation(rstd_c[:, t:t + 1], tB[:, t:t + 1],
                                    ACTF.Exp, bias=0.0, scale=cp_sb[:, 1:2])
            S.add("scalar", rstd2, waits=(f"var2_{t}",), signal=f"rstd2_{t}")

            def mkln2(e, t=t, ti=ti):
                return e.tensor_scalar(ln2tm[:, ti, :], q_dev[:, :],
                                       rstd_c[:, t:t + 1], None, ALU.mult)
            lnw = [f"rstd2_{t}"]
            if g >= 1:
                lnw.append(f"ln2fm_g{g - 1}")
            S.add("vector", mkln2, waits=tuple(lnw), signal=f"ln2_{t}")

    def emit_ln2fm(g):
        g0 = g * GT
        S.add("sync", (lambda e, g=g: [
            e.dma_start_transpose(ln2fm[:, :, tt * 128:(tt + 1) * 128],
                                  ln2tm[:, tt, :].rearrange(
                                      "p (c m) -> p c m", c=NK))
            for tt in range(GT)]),
            waits=(f"ln2_{g0}", f"ln2_{g0 + 1}") +
                  ((f"qkv_done_g{g - 1}",) if g >= 1 else ()),
            signal=f"ln2fm_g{g}", inc=GT * 16)

    def emit_qkv(g):
        g0 = g * GT
        qkv_tiles = {}
        for wi, wn in ((0, "q"), (1, "k"), (2, "v")):
            for ti in range(GT):
                t = g0 + ti
                for o in range(NO):
                    sig = f"{wn}ps_{t}_{o}"
                    ps, i = big_tile(
                        sig,
                        (lambda k, ti=ti: ln2fm[:, k,
                                                ti * 128:(ti + 1) * 128]),
                        (lambda k, wi=wi, o=o: w_sb[wi][
                            :, k, o * 512:(o + 1) * 512]),
                        waits=(f"ln2fm_g{g}", f"wcvt_{(wi * NK + NK) * 2 - 1}"))
                    qkv_tiles[(wn, ti, o)] = (sig, ps, i)
        S.add("tensor", lambda e: e.nop(),
              waits=(f"vps_{g0 + GT - 1}_{NO - 1}",),
              signal=f"qkv_done_g{g}")
        return qkv_tiles

    def emit_phi(g, qkv_tiles):
        g0 = g * GT
        for wn, dst in (("q", phq_tm), ("k", phk_tm)):
            for ti in range(GT):
                t = g0 + ti
                for o in range(NO):
                    sig, ps, i = qkv_tiles[(wn, ti, o)]
                    ew = [sig]
                    if o in last_e1:
                        ew.append(last_e1[o])
                    S.add("scalar", (lambda e, ps=ps, o=o: e.activation(
                        e1_sb[:, o * 512:(o + 1) * 512], ps[:, :], ACTF.Exp)),
                        waits=tuple(ew), signal=f"e1_{wn}_{t}_{o}")

                    def phi(e, ps=ps, o=o, dst=dst, ti=ti):
                        e.tensor_scalar(r_sb[:, o * 512:(o + 1) * 512],
                                        ps[:, :], 0.0, None, ALU.max)
                        return e.scalar_tensor_tensor(
                            dst[:, ti, o * 512:(o + 1) * 512],
                            e1_sb[:, o * 512:(o + 1) * 512], 1.0,
                            r_sb[:, o * 512:(o + 1) * 512], ALU.min, ALU.add)
                    pw = [f"e1_{wn}_{t}_{o}"]
                    if g >= 1:
                        pw.append(f"phifm_g{g - 1}")
                        if wn == "k":
                            pw += [f"kvps_{g0 - 1}_1", f"wps_{g0 - 1}_1"]
                    S.add("vector", phi, waits=tuple(pw),
                          signal=(f"bigfree_{i}", f"phi_{wn}_{t}_{o}"))
                    last_e1[o] = f"phi_{wn}_{t}_{o}"
        for ti in range(GT):
            t = g0 + ti
            for o in range(NO):
                sig, ps, i = qkv_tiles[("v", ti, o)]
                vw = [sig]
                if t >= 2:
                    vw.append(f"numdone_{t - 2}")
                S.add("scalar", (lambda e, ps=ps, o=o, ti=ti: e.copy(
                    vex[:, ti, 8 * o:8 * o + 8, 0:D], ps[:, :])),
                    waits=tuple(vw), signal=(f"bigfree_{i}", f"vcp_{t}_{o}"))

        S.add("vector", lambda e: e.engine_nop(),
              waits=tuple(f"phi_{wn}_{g0 + ti}_{o}" for wn in ("q", "k")
                          for ti in range(GT) for o in range(NO)),
              signal=f"phitm_g{g}")

    def emit_phifm(g):
        S.add("sync", (lambda e, g=g: [
            e.dma_start_transpose(dfm[:, :, tt * 128:(tt + 1) * 128],
                                  dtm[:, tt, :].rearrange(
                                      "p (c m) -> p c m", c=NK))
            for (dfm, dtm) in ((phq_fm, phq_tm), (phk_fm, phk_tm))
            for tt in range(GT)]),
            waits=(f"phitm_g{g}",) +
                  ((f"attn_done_g{g - 1}",) if g >= 1 else ()),
            signal=f"phifm_g{g}", inc=2 * GT * 16)

    def emit_attn(g):
        g0 = g * GT
        for ti in range(GT):
            t = g0 + ti

            w_sbf = [f"numdone_{t - 1}"] if t >= 1 else ["initS"]
            S.add("vector", (lambda e: e.tensor_copy(S_bf[:, :, :],
                                                     S_sb[:, :, :])),
                  waits=tuple(w_sbf), signal=f"sbf_{t}")

            for w in range(1):
                def kvw(e, ti=ti, w=w):
                    mm = None
                    for j in range(4):
                        hp = 4 * w + j
                        for hh in range(2):
                            h = 2 * hp + hh
                            mm = e.matmul(psK[hh * 64:hh * 64 + 64,
                                              j * 65:(j + 1) * 65],
                                          phk_tm[:, ti, h * D:(h + 1) * D],
                                          vex[:, ti, h, :], start=True,
                                          stop=True,
                                          tile_position=(0, hh * 64))
                    return mm
                wk = [f"phi_k_{t}_0", f"phi_k_{t}_1", f"vcp_{t}_0",
                      f"vcp_{t}_1"]
                if t >= 1 or w >= 1:
                    pt, pw = (t, 0) if w == 1 else (t - 1, 1)
                    wk.append(f"supd_{pt}_{pw}")
                S.add("tensor", kvw, waits=tuple(wk), signal=f"kvps_{t}_{w}")

            for w in range(2):
                def ww(e, ti=ti, w=w):
                    mm = None
                    for j in range(4):
                        hp = 4 * w + j
                        for hh in range(2):
                            h = 2 * hp + hh
                            mm = e.matmul(psW[hh * 64:hh * 64 + 64, j, :],
                                          phk_tm[:, ti, h * D:(h + 1) * D],
                                          mask_sb[:, :], start=True,
                                          stop=True,
                                          tile_position=(0, hh * 64))
                    return mm
                wk = [f"phi_k_{t}_0", f"phi_k_{t}_1"]
                if t >= 1 or w >= 1:
                    pt, pw = (t, 0) if w == 1 else (t - 1, 1)
                    wk.append(f"u_{pt}_{pw}")
                S.add("tensor", ww, waits=tuple(wk), signal=f"wps_{t}_{w}")

            for w in range(2):
                def uf(e, ti=ti, w=w):
                    for j in range(4):
                        hp = 4 * w + j
                        e.scalar_tensor_tensor(
                            u_sb[:, hp, :], psW[:, j, :],
                            S_sb[:, hp, D:D + 1],
                            phq_fm[:, hp, ti * 128:(ti + 1) * 128],
                            ALU.add, ALU.mult)
                    return e.engine_nop()
                wk = [f"wps_{t}_{w}", f"phifm_g{g}", f"sbf_{t}"]
                if t >= 1:
                    wk.append(f"densel_{t - 1}")
                S.add("vector", uf, waits=tuple(wk), signal=f"u_{t}_{w}")

            def supd0(e):
                return e.tensor_tensor(
                    S_sb[:, 0:4, :], S_sb[:, 0:4, :],
                    psK[:, 0:260].rearrange("p (j e) -> p j e", e=65),
                    ALU.add)
            S.add("vector", supd0,
                  waits=(f"kvps_{t}_0", f"u_{t}_0", f"u_{t}_1"),
                  signal=f"supd_{t}_0")

            for w in range(4):
                def atw(e, ti=ti, w=w):
                    ps = psT[w % 2]
                    hh = w % 2
                    mm = None
                    for j in range(4):
                        h = 8 * (w // 2) + 2 * j + hh
                        ce = h // 2
                        mm = e.matmul(ps[:, j, :],
                                      phk_fm[hh * 64:hh * 64 + 64, ce,
                                             ti * 128:(ti + 1) * 128],
                                      phq_fm[hh * 64:hh * 64 + 64, ce,
                                             ti * 128:(ti + 1) * 128],
                                      start=True, stop=True,
                                      tile_position=(hh * 64, 0))
                    return mm
                wk = [f"phifm_g{g}"]
                if t >= 1 or w >= 2:
                    pt, pw = (t, w - 2) if w >= 2 else (t - 1, w + 2)
                    wk.append(f"atm_{pt}_{pw}")
                S.add("tensor", atw, waits=tuple(wk), signal=f"atps_{t}_{w}")

            for w in range(4):
                def atmf(e, w=w):
                    ps = psT[w % 2]
                    return e.scalar_tensor_tensor(
                        atm[:, 4 * w:4 * w + 4, :], ps[:, :, :], 1.0,
                        mask_sb[:, :].rearrange("p (a c) -> p a c", a=1)
                        .broadcast_to([128, 4, 128]), ALU.mult, ALU.mult)
                wk = [f"atps_{t}_{w}"]
                if t >= 1:
                    wk.append(f"numdone_{t - 1}")
                S.add("vector", atmf, waits=tuple(wk), signal=f"atm_{t}_{w}")

            def supd1(e):
                return e.tensor_tensor(
                    S_sb[:, 4:8, :], S_sb[:, 4:8, :],
                    psK[:, 0:260].rearrange("p (j e) -> p j e", e=65),
                    ALU.add)
            S.add("vector", supd1, waits=(f"kvps_{t}_1",),
                  signal=f"supd_{t}_1")

            for w in range(1, 2):
                def kvw1(e, ti=ti, w=w):
                    mm = None
                    for j in range(4):
                        hp = 4 * w + j
                        for hh in range(2):
                            h = 2 * hp + hh
                            mm = e.matmul(psK[hh * 64:hh * 64 + 64,
                                              j * 65:(j + 1) * 65],
                                          phk_tm[:, ti, h * D:(h + 1) * D],
                                          vex[:, ti, h, :], start=True,
                                          stop=True,
                                          tile_position=(0, hh * 64))
                    return mm
                S.add("tensor", kvw1,
                      waits=(f"phi_k_{t}_0", f"phi_k_{t}_1", f"vcp_{t}_0",
                             f"vcp_{t}_1", f"supd_{t}_0"),
                      signal=f"kvps_{t}_1")

            def densel(e):
                mm = None
                for hp in range(8):
                    mm = e.matmul(psD[0:16, 0:128], sel_sb[:, hp, :],
                                  u_sb[:, hp, :], start=(hp == 0),
                                  stop=(hp == 7))
                return mm
            wk = [f"u_{t}_0", f"u_{t}_1"]
            if t >= 1:
                wk.append(f"rdenx_{t - 1}")
            S.add("tensor", densel, waits=tuple(wk), signal=f"densel_{t}")

            S.add("scalar", (lambda e: e.copy(den_f[:, :],
                                              psD[0:16, 0:128])),
                  waits=(f"densel_{t}",) + ((f"denT_{t - 1}",) if t >= 1
                                            else (f"wcvt_{4 * NK * 2 - 1}",)),
                  signal=f"denf_{t}")

            S.add("tensor", (lambda e: e.transpose(
                psD[:, 128:144], den_f[:, :], idn_sb[0:16, 0:16])),
                waits=(f"denf_{t}",), signal=f"denT_{t}")

            def rdenx(e, ti=ti):
                ssync(e, e.tensor_scalar(rden[:, ti, :], psD[:, 128:144],
                                         1e-6, None, ALU.add))
                return e.reciprocal(rden[:, ti, :], rden[:, ti, :])
            S.add("vector", rdenx, waits=(f"denT_{t}",), signal=f"rdenx_{t}")

            for w in range(2):
                def numw(e, ti=ti, w=w):
                    mm = None
                    for j in range(8):
                        h = 8 * w + j
                        ce, hh = h // 2, h % 2
                        slot = 4 * ((h % 2) + 2 * (h // 8)) + (h % 8) // 2
                        e.matmul(psN[:, j, :], atm[:, slot, :],
                                 vex[:, ti, h, 0:D], start=True, stop=False)
                        mm = e.matmul(psN[:, j, :],
                                      phq_fm[hh * 64:hh * 64 + 64, ce,
                                             ti * 128:(ti + 1) * 128],
                                      S_bf[hh * 64:hh * 64 + 64, h // 2,
                                           0:D],
                                      start=False, stop=True,
                                      tile_position=(hh * 64, 0))
                    return mm
                wk = [f"atm_{t}_{2 * w}", f"atm_{t}_{2 * w + 1}",
                      f"sbf_{t}", f"vcp_{t}_0", f"vcp_{t}_1",
                      f"phifm_g{g}"]
                if t >= 1 or w >= 1:
                    pt, pw = (t, 0) if w == 1 else (t - 1, 1)
                    wk.append(f"ynum_{pt}_{pw}")
                S.add("tensor", numw, waits=tuple(wk), signal=f"nps_{t}_{w}")
            S.add("tensor", lambda e: e.nop(),
                  waits=(f"nps_{t}_0", f"nps_{t}_1"), signal=f"numdone_{t}")

            for w in range(2):
                def ynum(e, ti=ti, w=w):
                    return e.tensor_tensor(
                        y_tm[:, ti, w * 512:(w + 1) * 512].rearrange(
                            "p (h d) -> p h d", d=D),
                        psN[:, :, :],
                        rden[:, ti, 8 * w:8 * w + 8].rearrange(
                            "p (h o) -> p h o", o=1).broadcast_to(
                                [128, 8, D]), ALU.mult)
                yw = [f"nps_{t}_{w}", f"rdenx_{t}"]
                if g >= 1:
                    yw.append(f"yfm_g{g - 1}")
                S.add("vector", ynum, waits=tuple(yw),
                      signal=f"ynum_{t}_{w}")

        S.add("vector", lambda e: e.engine_nop(),
              waits=(f"numdone_{g0 + GT - 1}",), signal=f"attn_done_g{g}")
        S.add("vector", lambda e: e.engine_nop(),
              waits=(f"ynum_{g0}_1", f"ynum_{g0 + 1}_1"),
              signal=f"ytm_g{g}")

    def emit_yfm(g):
        S.add("sync", (lambda e, g=g: [
            e.dma_start_transpose(y_fm[:, :, tt * 128:(tt + 1) * 128],
                                  y_tm[:, tt, :].rearrange(
                                      "p (c m) -> p c m", c=NK))
            for tt in range(GT)]),
            waits=(f"ytm_g{g}",) +
                  ((f"p_done_g{g - 1}",) if g >= 1 else ()),
            signal=f"yfm_g{g}", inc=GT * 16)

    def emit_P(g):
        g0 = g * GT
        p_tiles = []
        for ti in range(GT):
            t = g0 + ti
            for o in range(NO):
                sig = f"pps_{t}_{o}"
                ps, i = big_tile(
                    sig,
                    (lambda k, ti=ti: y_fm[:, k, ti * 128:(ti + 1) * 128]),
                    (lambda k, o=o: w_sb[3][:, k, o * 512:(o + 1) * 512]),
                    waits=(f"yfm_g{g}", f"wcvt_{(3 * NK + NK) * 2 - 1}",
                           "init_v"),
                    outer=(lambda o=o: (ones_sb[0:1, :],
                                        bp_sb[0:1, o * 512:(o + 1) * 512])))
                p_tiles.append((t, sig, ps, i, o))
        S.add("tensor", lambda e: e.nop(),
              waits=(f"pps_{g0 + GT - 1}_{NO - 1}",),
              signal=f"p_done_g{g}")

        for t, sig, ps, i, o in p_tiles:
            def pev(e, ps=ps, t=t, o=o):
                ssync(e, e.scalar_tensor_tensor(
                    x_sb[:, t % RT, o * 512:(o + 1) * 512], ps[:, :], 0.0,
                    x_sb[:, t % RT, o * 512:(o + 1) * 512], ALU.add, ALU.add,
                    accum_out=tA[:, t:t + 1]))
                if o == 0:
                    return e.tensor_copy(sxC[:, t:t + 1], tA[:, t:t + 1])
                return e.tensor_tensor(sxC[:, t:t + 1], sxC[:, t:t + 1],
                                       tA[:, t:t + 1], ALU.add)
            S.add("vector", pev, waits=(sig,),
                  signal=(f"bigfree_{i}", f"pev_{t}_{o}"))

    def emit_store(g):
        s0 = (g * GT) % RT
        S.add("sync", (lambda e, g=g, s0=s0: [
            e.dma_start(out=out_h.rearrange("p (t e) -> p t e", e=E)[
                :, g * GT:(g + 1) * GT, :],
                in_=x_sb[:, s0:s0 + GT, :])]),
            waits=tuple(f"ffnev_f2_{g * GT + ti}_{o}"
                        for ti in range(GT) for o in range(NO)),
            signal=f"store_{g}", inc=16)

    def emit_w(lo, hi):
        for li, wts, fn in wl_list[lo:hi]:
            S.add("sync", fn, waits=tuple(wts), signal=f"wload_{li}", inc=16)
        for li, fn in wc_list[lo:hi]:
            S.add("scalar", fn, waits=(f"wload_{li}",), signal=f"wcvt_{li}")

    emit_w(0, 48)
    t_f1 = {}
    emit_sums(0)
    emit_sums(1)
    emit_quant("f1", 0, sxA)
    t_f1[0] = emit_fT_mm("f1", 0)
    emit_quant("f1", 1, sxA)
    for t, tl in t_f1[0]:
        ffn_evac(t, "f1", tl, sxB)
    t_f1[1] = emit_fT_mm("f1", 1)
    emit_ln2(0)
    for t, tl in t_f1[1]:
        ffn_evac(t, "f1", tl, sxB)
    emit_ln2fm(0)
    qk0 = emit_qkv(0)
    emit_phi(0, qk0)
    emit_w(48, 64)
    S.add("scalar", (lambda e: e.activation(S_sb[:, :, :], S_sb[:, :, :],
                                            ACTF.Copy, scale=0.0, bias=0.0)),
          waits=(f"wcvt_{4 * NK * 2 - 1}",), signal="initS")
    emit_phifm(0)

    t_f2 = {}
    for g in range(NG):
        if g + 2 < NG:
            emit_sums(g + 2)
            emit_quant("f1", g + 2, sxA)
            emit_fT_cast("f1", g + 2)
        if g >= 1 and g + 1 < NG:
            for t, tl in t_f1[g + 1]:
                ffn_evac(t, "f1", tl, sxB)
        if g >= 1:
            for t, tl in t_f2[g - 1]:
                ffn_evac(t, "f2", tl, None)
            emit_store(g - 1)
            gl = g - 1 + RT // GT
            if gl < NG:
                emit_xload(gl, waits=(f"store_{g - 1}",))
        if g + 1 < NG:
            emit_ln2(g + 1)
            emit_ln2fm(g + 1)
        emit_attn(g)
        if g + 1 < NG:
            qk = emit_qkv(g + 1)
            emit_phi(g + 1, qk)
        emit_yfm(g)
        emit_P(g)
        if g + 2 < NG:
            t_f1[g + 2] = emit_fmm("f1", g + 2)
        emit_quant("f2", g, sxC)
        t_f2[g] = emit_fT_mm("f2", g)
        if g + 1 < NG:
            emit_phifm(g + 1)

    for t, tl in t_f2[NG - 1]:
        ffn_evac(t, "f2", tl, None)
    emit_store(NG - 1)

    S.add("sync", lambda e: e.nop(),
          waits=tuple(f"store_{g}" for g in range(NG)), signal=None)

    S.resolve()
    sems = {name: nc.alloc_semaphore(f"sem_{name}")
            for name in ENGS + ("gpsimd_dma",)}
    with nc.Block() as block:
        S.emit(block, sems)
    return nc


_NC_CACHE = {}


def _get_nc():
    if "nc" not in _NC_CACHE:
        _NC_CACHE["nc"] = build_program()
    return _NC_CACHE["nc"]


def _consts():
    s, c = np.meshgrid(np.arange(128), np.arange(128), indexing="ij")
    mask = (s <= c).astype(ml_dtypes.bfloat16)
    idn = np.eye(128, dtype=np.float32)
    sel = np.zeros((128, 8, 16), dtype=ml_dtypes.bfloat16)
    for p in range(128):
        for hp in range(8):
            sel[p, hp, 2 * hp + (p // 64)] = 1.0
    ones = np.ones((1, 128), dtype=ml_dtypes.bfloat16)
    cp = np.zeros((128, 4), dtype=np.float32)
    cp[:, 0] = -1.0
    cp[:, 1] = -0.5
    cp[:, 2] = 80.5
    cp[:, 3] = -80.0
    return mask, idn, sel, ones, cp


def _layout_w(wT):
    return np.ascontiguousarray(
        wT.reshape(NK, 128, 2, 512).transpose(0, 2, 1, 3).reshape(2 * E,
                                                                  E // 2))


def _layout_x(x):
    return np.ascontiguousarray(
        x.reshape(NT, 128, E).transpose(1, 0, 2).reshape(128, NT * E))


def _unlayout_out(o):
    return np.ascontiguousarray(
        o.reshape(128, NT, E).transpose(1, 0, 2).reshape(T, E))


def kernel(L0_attn_x, L0_attn_Wq, L0_attn_Wk, L0_attn_Wv, L0_attn_Wp,
           L0_attn_bp, L0_ffn_weight, L0_ffn_weight_scale, L0_ffn_bias):
    nc = _get_nc()
    mask, idn, sel, ones, cp = _consts()
    x = np.ascontiguousarray(np.asarray(L0_attn_x, dtype=np.float32))
    wqT = _layout_w(np.asarray(L0_attn_Wq, np.float32).T)
    wkT = _layout_w(np.asarray(L0_attn_Wk, np.float32).T)
    wvT = _layout_w(np.asarray(L0_attn_Wv, np.float32).T)
    wpT = _layout_w(np.asarray(L0_attn_Wp, np.float32).T)
    w8 = np.asarray(L0_ffn_weight)
    w8T = np.ascontiguousarray(w8.view(np.uint8).T)
    bp = np.ascontiguousarray(np.asarray(L0_attn_bp, np.float32).reshape(1, E))
    fb = np.ascontiguousarray(np.asarray(L0_ffn_bias, np.float32).reshape(1, E))
    ws = np.full((128, 1), np.float32(np.asarray(L0_ffn_weight_scale)),
                 dtype=np.float32)

    shared = {"wqT": wqT, "wkT": wkT, "wvT": wvT, "wpT": wpT, "w8T": w8T,
              "bp": bp, "fb": fb, "ws": ws, "cp": cp, "maskT": mask,
              "idn": idn, "sel": sel, "ones": ones}
    in_maps = [dict(shared, x=_layout_x(x[b])) for b in range(8)]
    res = run_bass_kernel_spmd(nc, in_maps, core_ids=list(range(8)))
    out = np.stack([_unlayout_out(np.asarray(r["out"])) for r in res.results],
                   axis=0)
    return out.astype(np.float32)
